# version 14
# baseline (speedup 1.0000x reference)
import sys
import numpy as np
import ml_dtypes
from contextlib import ExitStack

sys.path.insert(0, "/opt/trn_rl_repo")
import concourse.bass as bass
import concourse.mybir as mybir
from concourse.bass_utils import run_bass_kernel_spmd

F32 = mybir.dt.float32
BF16 = mybir.dt.bfloat16
I32 = mybir.dt.int32
U32 = mybir.dt.uint32
AF = mybir.ActivationFunctionType
ALU = mybir.AluOpType
AX = mybir.AxisListType

NCORES = 8
R = 1024
D = 4096
DI = 14336
NT = R // 128
NH = D // 128
NI = DI // 128
NF2 = 2048
EPS = 1e-5

import os
_STAGE = 3
_SUB = os.environ.get("K_SUB", "")
_NQ = int(os.environ.get("K_NQ", "8"))
_NFT = int(os.environ.get("K_NFT", "4"))
_NOTR = os.environ.get("K_NOTR", "")
_MIX = os.environ.get("K_MIX", "full")
_DIAG = False


class Sem:
    def __init__(self, ctx, nc, name):
        self.h = ctx.enter_context(nc.semaphore(name))
        self.v = 0

    def inc(self, n=1):
        self.v += n
        return self.v

    def bump(self, n=1):
        self.v += n
        return n


def build_program(stage):
    nc = bass.Bass("TRN2", target_bir_lowering=False)
    ctx = ExitStack()

    x_d = nc.dram_tensor("x", [R, D], F32, kind="ExternalInput").ap()
    idb_d = nc.dram_tensor("idb", [128, 128], BF16, kind="ExternalInput").ap()
    out_d = nc.dram_tensor("out", [R, D], F32, kind="ExternalOutput").ap()
    xdup_d = nc.dram_tensor("xdup", [R, 2 * D], BF16, kind="Internal").ap()
    mix_d = nc.dram_tensor("mix", [R, D], BF16, kind="Internal").ap()
    corr_d = nc.dram_tensor("corrd", [R, D], BF16, kind="Internal").ap()
    xnt_d = nc.dram_tensor("xntd", [128, NH * R], BF16, kind="Internal").ap()
    if stage >= 2:
        cfwd_d = nc.dram_tensor("cfwd", [8, D, 512], BF16, kind="ExternalInput").ap()
        c2_d = nc.dram_tensor("c2", [4, NF2, 1024], BF16, kind="ExternalInput").ap()
    if stage >= 3:
        idf_d = nc.dram_tensor("idf", [128, 128], F32, kind="ExternalInput").ap()
        gwT_d = nc.dram_tensor("gwT", [DI, D], I32, kind="ExternalInput").ap()
        uwT_d = nc.dram_tensor("uwT", [DI, D], I32, kind="ExternalInput").ap()
        dwT_d = nc.dram_tensor("dwT", [D, DI], I32, kind="ExternalInput").ap()
        guv_d = nc.dram_tensor("guv", [DI, 128], I32, kind="ExternalInput").ap()
        dsT_d = nc.dram_tensor("dsT", [D, NI], F32, kind="ExternalInput").ap()
        dzT_d = nc.dram_tensor("dzT", [D, NI], I32, kind="ExternalInput").ap()
        h_d = nc.dram_tensor("hd", [DI, R], BF16, kind="Internal").ap()
        dl_d = nc.dram_tensor("dld", [32, 128, DI], BF16, kind="Internal").ap()

    idb = ctx.enter_context(nc.sbuf_tensor("idb_s", [128, 128], BF16))
    pnyq = ctx.enter_context(nc.sbuf_tensor("pnyq", [128, NT], F32))
    epsc = ctx.enter_context(nc.sbuf_tensor("epsc", [128, 1], F32))
    idf_s = ctx.enter_context(nc.sbuf_tensor("idfp", [128, 128], F32))
    xnt_cm = nc.sbuf_tensor("xnt", [128, NH * R], BF16)
    xnt = xnt_cm.__enter__()
    xnt3 = xnt[:].rearrange("p (c t) -> p c t", c=NH)
    ppr_cm = nc.sbuf_tensor("ppr", [128, 16 * 1024], BF16)
    ppr = ppr_cm.__enter__()
    ppr3 = ppr[:].rearrange("p (fc t) -> p fc t", fc=16)

    sems = {}

    def sem(name):
        if name not in sems:
            sems[name] = Sem(ctx, nc, "s_" + name)
        return sems[name]

    ldx = sem("ldx")
    vvar = sem("vvar")
    srst = sem("srst")
    axnb = sem("axnb")
    adup = sem("adup")
    atr = sem("atr")
    aev = sem("aev")
    aout = sem("aout")

    with (
        nc.sbuf_tensor("xa", [128, 2 * D], F32) as xa,
        nc.sbuf_tensor("xnf", [128, D], F32) as xnf,
        nc.sbuf_tensor("xnb", [128, D], BF16) as xnb,
        nc.sbuf_tensor("sta", [128, 16], F32) as sta,
        nc.sbuf_tensor("xb", [128, D], BF16) as xb,
        nc.sbuf_tensor("cfb", [128, 2 * 16 * 512], BF16) as cfb,
        nc.sbuf_tensor("sqb", [128, 4 * 512], F32) as sqb,
        nc.psum_tensor("pstA", [128, 2 * 1024], BF16) as pstA,
        nc.psum_tensor("psB4", [128, 2048], F32) as psB4,
        nc.Block() as blk,
    ):
        cfb4 = cfb[:].rearrange("p (s c w) -> p s c w", s=2, c=16)
        bldp = [sem("bld0"), sem("bld1")]
        bmm, bsq, bev = sem("bmm"), sem("bsq"), sem("bev")
        pstA3 = pstA[:].rearrange("p (b c t) -> p b c t", b=2, c=8)

        amw = sem("amw")
        axb = sem("axb")

        @blk.sync
        def _(sync):
            sync.dma_start(out=idb[:], in_=idb_d).then_inc(ldx.h, ldx.bump(16))
            if stage >= 3:
                sync.dma_start(out=idf_s[:], in_=idf_d).then_inc(
                    ldx.h, ldx.bump(16))
            for tt in range(NT):
                sync.dma_start(out=xa[:, (tt % 2) * D:(tt % 2 + 1) * D],
                               in_=x_d[tt * 128:(tt + 1) * 128, :]
                               ).then_inc(ldx.h, ldx.bump(16))
                sync.wait_ge(axnb.h, tt + 1)
                sync.dma_start(out=xdup_d[tt * 128:(tt + 1) * 128, 0:D],
                               in_=xnb[:]).then_inc(adup.h, adup.bump(16))
                sync.dma_start(out=xdup_d[tt * 128:(tt + 1) * 128, D:2 * D],
                               in_=xnb[:]).then_inc(adup.h, adup.bump(16))
                if stage == 1:
                    sync.dma_start(out=out_d[tt * 128:(tt + 1) * 128, :],
                                   in_=xnf[:]).then_inc(aout.h, aout.bump(16))
                if stage >= 3:
                    sync.wait_ge(axb.h, tt + 1)
                    sync.dma_start(out=mix_d[tt * 128:(tt + 1) * 128, :],
                                   in_=xb[:]).then_inc(amw.h, amw.bump(16))

        vch = sem("vch")
        sch = sem("sch")

        @blk.scalar
        def _(scalar):
            for tt in range(NT):
                scalar.wait_ge(vvar.h, tt + 1)
                scalar.activation(sta[:, 2:3], sta[:, 1:2], AF.Ln,
                                  bias=epsc[:, 0:1]).then_inc(sch.h, sch.bump(1))
                scalar.wait_ge(sch.h, sch.v)
                scalar.activation(sta[:, 3:4], sta[:, 2:3], AF.Exp,
                                  scale=-0.5).then_inc(srst.h, srst.bump(1))

        ldx0 = 32 if stage >= 3 else 16

        @blk.vector
        def _(vector):
            def edge(inst):
                inst.then_inc(vch.h, vch.bump(1))
                vector.wait_ge(vch.h, vch.v)

            vector.memset(epsc[:], EPS)
            for tt in range(NT):
                xt = xa[:, (tt % 2) * D:(tt % 2 + 1) * D]
                vector.wait_ge(ldx.h, 16 * (tt + 1) + ldx0)
                edge(vector.reduce_sum(sta[:, 4:5], xt, axis=AX.X))
                edge(vector.tensor_scalar(sta[:, 0:1], sta[:, 4:5], 1.0 / D, None,
                                          op0=ALU.mult))
                edge(vector.tensor_tensor(xnf[:], xt, xt, op=ALU.mult))
                edge(vector.reduce_sum(sta[:, 11:12], xnf[:], axis=AX.X))
                edge(vector.tensor_scalar(sta[:, 5:6], sta[:, 11:12], 1.0 / D, None,
                                          op0=ALU.mult))
                edge(vector.tensor_tensor(sta[:, 6:7], sta[:, 0:1], sta[:, 0:1],
                                          op=ALU.mult))
                vector.tensor_tensor(sta[:, 1:2], sta[:, 5:6], sta[:, 6:7],
                                     op=ALU.subtract).then_inc(vvar.h, vvar.bump(1))
                vector.wait_ge(srst.h, tt + 1)
                if tt > 0:
                    vector.wait_ge(atr.h, 4 * tt)
                    vector.wait_ge(adup.h, 32 * tt)
                    if stage == 1:
                        vector.wait_ge(aout.h, 16 * tt)
                edge(vector.tensor_scalar(xnf[:], xt, sta[:, 0:1], sta[:, 3:4],
                                          op0=ALU.subtract, op1=ALU.mult))
                vector.tensor_copy(xnb[:], xnf[:]).then_inc(axnb.h, axnb.bump(1))
                if stage >= 3:
                    if tt > 0:
                        vector.wait_ge(amw.h, 16 * tt)
                    vector.tensor_copy(xb[:], xt).then_inc(axb.h, axb.bump(1))
                edge(vector.reduce_sum(sta[:, 7:8], xnf[:, 0:D:2], axis=AX.X))
                edge(vector.reduce_sum(sta[:, 8:9], xnf[:, 1:D:2], axis=AX.X))
                edge(vector.tensor_tensor(sta[:, 9:10], sta[:, 7:8], sta[:, 8:9],
                                          op=ALU.subtract))
                edge(vector.tensor_tensor(sta[:, 10:11], sta[:, 9:10], sta[:, 9:10],
                                          op=ALU.mult))
                vector.tensor_scalar(pnyq[:, tt:tt + 1], sta[:, 10:11],
                                     1.0 / D, None, op0=ALU.mult)
                for r in range(4):
                    vector.wait_ge(atr.h, 4 * tt + r + 1)
                    src = pstA3[:, r % 2, :, :]
                    dst = xnt3[:, r * 8:(r + 1) * 8, tt * 128:(tt + 1) * 128]
                    vector.tensor_copy(dst, src).then_inc(aev.h, aev.bump(1))

        @blk.tensor
        def _(tensor):
            for tt in range(NT):
                tensor.wait_ge(axnb.h, tt + 1)
                for r in range(4):
                    gr = 4 * tt + r
                    if gr >= 2:
                        tensor.wait_ge(aev.h, gr - 1)
                    bank = pstA3[:, r % 2, :, :]
                    for k in range(8):
                        c = r * 8 + k
                        t = tensor.transpose(bank[:, k, :],
                                             xnb[:, c * 128:(c + 1) * 128],
                                             idb[:])
                        if k == 7:
                            t.then_inc(atr.h, atr.bump(1))

        if stage >= 2:
            @blk.sync
            def _(sync):
                for g in range(8):
                    for jh in range(2):
                        gh = 2 * g + jh
                        if gh >= 2:
                            sync.wait_ge(bmm.h, 2 * (gh - 1))
                        sync.dma_start(
                            out=cfb4[:, gh % 2, :, :],
                            in_=cfwd_d[g][jh * 2048:(jh + 1) * 2048, :]
                            .rearrange("(c p) w -> p c w", p=128)
                        ).then_inc(bldp[gh % 2].h, bldp[gh % 2].bump(16))
                if stage >= 3:
                    sync.wait_ge(sems["amw"].h, sems["amw"].v)
                sync.wait_ge(adup.h, adup.v)

            @blk.tensor
            def _(tensor):
                for ftg in range(16):
                    g, fhalf = divmod(ftg, 2)
                    for ts in range(2):
                        pi = ftg * 2 + ts
                        if pi >= 2:
                            tensor.wait_ge(bsq.h, pi - 1)
                        for j in range(NH):
                            jh = j // 16
                            gh = 2 * g + jh
                            tensor.wait_ge(bldp[gh % 2].h, 16 * (gh // 2 + 1))
                            tensor.wait_ge(aev.h,
                                           min(32, 16 * ts + 13 + j // 8))
                            mm = tensor.matmul(
                                psB4[:, ts * 1024:ts * 1024 + 512],
                                cfb4[:, gh % 2, j % 16,
                                     fhalf * 128:(fhalf + 1) * 128],
                                xnt3[:, j, ts * 512:(ts + 1) * 512],
                                start=(j == 0), stop=(j == NH - 1))
                            mm = tensor.matmul(
                                psB4[:, ts * 1024 + 512:(ts + 1) * 1024],
                                cfb4[:, gh % 2, j % 16,
                                     256 + fhalf * 128:256 + (fhalf + 1) * 128],
                                xnt3[:, j, ts * 512:(ts + 1) * 512],
                                start=(j == 0), stop=(j == NH - 1))
                        mm.then_inc(bmm.h, bmm.bump(1))

            @blk.scalar
            def _(scalar):
                for pi in range(32):
                    scalar.wait_ge(bmm.h, pi + 1)
                    if pi >= 2:
                        scalar.wait_ge(bev.h, pi - 1)
                    ts = pi % 2
                    sl = (pi % 2) * 1024
                    scalar.activation(sqb[:, sl:sl + 512],
                                      psB4[:, ts * 1024:ts * 1024 + 512],
                                      AF.Square)
                    scalar.activation(sqb[:, sl + 512:sl + 1024],
                                      psB4[:, ts * 1024 + 512:(ts + 1) * 1024],
                                      AF.Square).then_inc(bsq.h, bsq.bump(1))

            @blk.vector
            def _(vector):
                for pi in range(32):
                    vector.wait_ge(bsq.h, pi + 1)
                    ftg, ts = divmod(pi, 2)
                    sl = (pi % 2) * 1024
                    vector.tensor_tensor(
                        ppr3[:, ftg, ts * 512:(ts + 1) * 512],
                        sqb[:, sl:sl + 512],
                        sqb[:, sl + 512:sl + 1024],
                        op=ALU.add).then_inc(bev.h, bev.bump(1))

    if stage >= 2:
        vch = sems["vch"]
        if True:
            if True:
                cld, cmm, cev = sem("cld"), sem("cmm"), sem("cev")
                cwrp = [sem("cwr0"), sem("cwr1")]
                with (
                    nc.sbuf_tensor("c2q", [128, 16 * 1024], BF16) as c2q,
                    nc.sbuf_tensor("cst", [128, 2 * 1024], BF16) as cst,
                    nc.psum_tensor("psC", [128, 2048], F32) as psC,
                    nc.Block() as blk,
                ):
                    c2q3 = c2q[:].rearrange("p (fc l) -> p fc l", fc=16)

                    @blk.sync
                    def _(sync):
                        for q in range(4):
                            sync.wait_ge(cmm.h, 8 * q)
                            sync.dma_start(
                                out=c2q3,
                                in_=c2_d[q].rearrange("(c p) l -> p c l", p=128)
                            ).then_inc(cld.h, cld.bump(16))
                            for tt in range(NT):
                                gl = q * 8 + tt
                                sync.wait_ge(cev.h, gl + 1)
                                sync.dma_start(
                                    out=corr_d[tt * 128:(tt + 1) * 128,
                                               q * 1024:(q + 1) * 1024],
                                    in_=cst[:, (gl % 2) * 1024:
                                            (gl % 2 + 1) * 1024]
                                ).then_inc(cwrp[gl % 2].h, cwrp[gl % 2].bump(16))
                        sync.wait_ge(cwrp[0].h, 16 * 16)
                        sync.wait_ge(cwrp[1].h, 16 * 16)

                    @blk.tensor
                    def _(tensor):
                        for q in range(4):
                            tensor.wait_ge(cld.h, 16 * (q + 1))
                            for tt in range(NT):
                                gl = q * 8 + tt
                                if gl >= 2:
                                    tensor.wait_ge(cev.h, gl - 1)
                                for fc in range(16):
                                    lhsT = ppr3[:, fc, tt * 128:(tt + 1) * 128]
                                    for lt in range(2):
                                        b = (tt % 2) * 2 + lt
                                        mm = tensor.matmul(
                                            psC[:, b * 512:(b + 1) * 512],
                                            lhsT,
                                            c2q3[:, fc, lt * 512:(lt + 1) * 512],
                                            start=(fc == 0), stop=(fc == 15))
                                mm.then_inc(cmm.h, cmm.bump(1))

                    @blk.vector
                    def _(vector):
                        def edge(inst):
                            inst.then_inc(vch.h, vch.bump(1))
                            vector.wait_ge(vch.h, vch.v)

                        for q in range(4):
                            for tt in range(NT):
                                gl = q * 8 + tt
                                vector.wait_ge(cmm.h, gl + 1)
                                if gl >= 2:
                                    vector.wait_ge(cwrp[gl % 2].h, 16 * (gl // 2))
                                cb = (gl % 2) * 1024
                                for lt in range(2):
                                    b = (tt % 2) * 2 + lt
                                    ins = vector.tensor_copy(
                                        cst[:, cb + lt * 512:cb + (lt + 1) * 512],
                                        psC[:, b * 512:(b + 1) * 512])
                                edge(ins)
                                vector.tensor_scalar(
                                    cst[:, cb:cb + 1024:2],
                                    cst[:, cb:cb + 1024:2],
                                    pnyq[:, tt:tt + 1], None, op0=ALU.add)
                                vector.tensor_scalar(
                                    cst[:, cb + 1:cb + 1024:2],
                                    cst[:, cb + 1:cb + 1024:2],
                                    pnyq[:, tt:tt + 1], None, op0=ALU.subtract
                                ).then_inc(cev.h, cev.bump(1))

        ppr_cm.__exit__(None, None, None)
        if stage == 2:
            dxl, dini, dexp, dsm = sem("dxl"), sem("dini"), sem("dexp"), sem("dsm")
            did, giot, dga, dack, dwr = (sem("did"), sem("giot"), sem("dga"),
                                         sem("dack"), sem("dwr"))
            WOUT = 16
            if _DIAG:
                diag_d = nc.dram_tensor("diag", [R, 16], F32,
                                        kind="ExternalOutput").ap()
                ddg = sem("ddg")
            dcl = sem("dcl")
            with (
                nc.sbuf_tensor("crb2", [128, D], BF16) as crb2,
                nc.sbuf_tensor("dgs", [128, 16], F32) as dgs,
                nc.sbuf_tensor("m8", [128, 8], F32) as m8,
                nc.sbuf_tensor("f8", [128, 8], F32) as f8,
                nc.sbuf_tensor("e8f", [128, 8], F32) as e8f,
                nc.sbuf_tensor("p8", [128, 8], F32) as p8,
                nc.sbuf_tensor("d8", [128, 8], U32) as d8,
                nc.sbuf_tensor("idx8", [128, 16], U32) as idx8,
                nc.sbuf_tensor("rowb", [128, 1], F32) as rowb,
                nc.sbuf_tensor("gb", [128, 8192], BF16) as gb,
                nc.sbuf_tensor("acc", [128, D], F32) as acc,
                nc.sbuf_tensor("xf2", [128, 2 * D], F32) as xf2,
                nc.sbuf_tensor("s1r", [128, 2], F32) as s1r,
                nc.Block() as blk,
            ):
                xdup_flat = xdup_d.rearrange("r (c one) -> (r c) one", one=1)

                @blk.sync
                def _(sync):
                    for tt in range(NT):
                        if tt >= 1:
                            sync.wait_ge(did.h, tt)
                        sync.dma_start(out=crb2[:],
                                       in_=corr_d[tt * 128:(tt + 1) * 128, :]
                                       ).then_inc(dcl.h, dcl.bump(16))
                        if tt >= 2:
                            sync.wait_ge(dini.h, tt - 1)
                        sync.dma_start(out=xf2[:, (tt % 2) * D:(tt % 2 + 1) * D],
                                       in_=x_d[tt * 128:(tt + 1) * 128, :]
                                       ).then_inc(dxl.h, dxl.bump(16))
                        sync.wait_ge(dack.h, 8 * (tt + 1))
                        if _DIAG:
                            sync.dma_start(out=diag_d[tt * 128:(tt + 1) * 128, :],
                                           in_=dgs[:]).then_inc(ddg.h, ddg.bump(16))
                        sync.dma_start(out=out_d[tt * 128:(tt + 1) * 128, :],
                                       in_=acc[:]).then_inc(dwr.h, dwr.bump(16))

                @blk.scalar
                def _(scalar):
                    for tt in range(NT):
                        scalar.wait_ge(dxl.h, 16 * (tt + 1))
                        if tt > 0:
                            scalar.wait_ge(dwr.h, WOUT * tt)
                        scalar.activation(acc[:], xf2[:, (tt % 2) * D:(tt % 2 + 1) * D],
                                          AF.Identity).then_inc(dini.h, dini.bump(1))
                        scalar.wait_ge(dexp.h, tt + 1)
                        scalar.activation(e8f[:], f8[:], AF.Exp
                                          ).then_inc(dsm.h, dsm.bump(1))

                @blk.gpsimd
                def _(gpsimd):
                    for tt in range(NT):
                        gpsimd.wait_ge(did.h, tt)
                        gpsimd.iota(rowb[:], [[0, 1]], base=tt * 128 * 8192,
                                    channel_multiplier=8192,
                                    allow_small_or_imprecise_dtypes=True
                                    ).then_inc(giot.h, giot.bump(1))
                        for k in range(8):
                            gk = 8 * tt + k
                            gpsimd.wait_ge(did.h, tt + 1)
                            if gk >= 2:
                                gpsimd.wait_ge(dack.h, gk - 1)
                            gpsimd.indirect_dma_start(
                                out=gb[:, (k % 2) * D:(k % 2 + 1) * D],
                                out_offset=None,
                                in_=xdup_flat,
                                in_offset=bass.IndirectOffsetOnAxis(
                                    ap=idx8[:, (tt % 2) * 8 + k:(tt % 2) * 8 + k + 1],
                                    axis=0),
                            ).then_inc(dga.h, dga.bump(16))

                @blk.vector
                def _(vector):
                    def edge(inst):
                        inst.then_inc(vch.h, vch.bump(1))
                        vector.wait_ge(vch.h, vch.v)

                    for tt in range(NT):
                        cs = crb2[:]
                        vector.wait_ge(dcl.h, 16 * (tt + 1))
                        edge(vector.max(out=m8[:], in_=cs))
                        edge(vector.max_index(out=d8[:], in_max=m8[:], in_values=cs))
                        vector.wait_ge(giot.h, tt + 1)
                        vector.tensor_scalar(
                            idx8[:, (tt % 2) * 8:(tt % 2) * 8 + 8], d8[:],
                            rowb[:, 0:1], None, op0=ALU.add
                        ).then_inc(did.h, did.bump(1))
                        edge(vector.tensor_scalar(f8[:], m8[:], m8[:, 0:1], None,
                                                  op0=ALU.subtract))
                        vector.nop().then_inc(dexp.h, dexp.bump(1))
                        vector.wait_ge(dsm.h, tt + 1)
                        edge(vector.reduce_sum(s1r[:, 0:1], e8f[:], axis=AX.X))
                        edge(vector.reciprocal(s1r[:, 1:2], s1r[:, 0:1]))
                        edge(vector.tensor_scalar(p8[:], e8f[:], s1r[:, 1:2], None,
                                                  op0=ALU.mult))
                        if _DIAG:
                            if tt > 0:
                                vector.wait_ge(ddg.h, 16 * tt)
                            edge(vector.tensor_copy(dgs[:, 0:8], d8[:]))
                            edge(vector.tensor_copy(dgs[:, 8:16], p8[:]))
                        vector.wait_ge(dini.h, tt + 1)
                        for k in range(8):
                            gk = 8 * tt + k
                            vector.wait_ge(dga.h, 16 * (gk + 1))
                            edge(vector.scalar_tensor_tensor(
                                out=acc[:], in0=gb[:, (k % 2) * D:(k % 2 + 1) * D],
                                scalar=p8[:, k:k + 1], in1=acc[:],
                                op0=ALU.mult, op1=ALU.add))
                            vector.nop().then_inc(dack.h, dack.bump(1))

    if stage >= 3:
        vch = sems["vch"]
        eldp = [sem("eld0"), sem("eld1")]
        ezs, edqv, edqa = sem("ezs"), sem("edqv"), sem("edqa")
        etr, eevg, eevu, emm = sem("etr"), sem("eevg"), sem("eevu"), sem("emm")
        esil, ehm = sem("esil"), sem("ehm")
        ehwp = [sem("ehw0"), sem("ehw1")]
        mcl, mtk, mex, msm, mid = (sem("mcl"), sem("mtk"), sem("mex"),
                                   sem("msm"), sem("mid"))
        mgi, mga, mack, mwr = sem("mgi"), sem("mga"), sem("mack"), sem("mwr")
        mxl = sem("mxl")
        ectx = ExitStack()
        _e = ectx.enter_context
        wg2 = _e(nc.sbuf_tensor("wg2", [128, 2 * D], I32))
        wu2 = _e(nc.sbuf_tensor("wu2", [128, 2 * D], I32))
        guv2 = _e(nc.sbuf_tensor("guv2", [128, 256], I32))
        zsg = _e(nc.sbuf_tensor("zsg", [128, 64], F32))
        zsu = _e(nc.sbuf_tensor("zsu", [128, 64], F32))
        nzsu = _e(nc.sbuf_tensor("nzsu", [128, 64], F32))
        dqg = _e(nc.sbuf_tensor("dqg", [128, D], BF16))
        dqu = _e(nc.sbuf_tensor("dqu", [128, D], BF16))
        lg = _e(nc.sbuf_tensor("lg", [128, D], BF16))
        lu = _e(nc.sbuf_tensor("lu", [128, D], BF16))
        sg = _e(nc.sbuf_tensor("sg", [128, 1024], BF16))
        hb = _e(nc.sbuf_tensor("hb", [128, 2 * 1024], BF16))
        crb = _e(nc.sbuf_tensor("crb", [128, D], BF16))
        macc = _e(nc.sbuf_tensor("macc", [128, D], BF16))
        xsd = _e(nc.sbuf_tensor("xsd", [128, D], BF16))
        gb1 = _e(nc.sbuf_tensor("gb1", [128, D], BF16))
        mv8 = _e(nc.sbuf_tensor("mv8", [128, 32], F32))
        mi8 = _e(nc.sbuf_tensor("mi8", [128, 16], U32))
        mrow = _e(nc.sbuf_tensor("mrow", [128, 4], F32))
        pstE = _e(nc.psum_tensor("pstE", [128, 4 * 1024], BF16))
        psg = _e(nc.psum_tensor("psg", [128, 1024], F32))
        psu = _e(nc.psum_tensor("psu", [128, 1024], F32))
        with nc.Block() as blk:
            m8v = mv8[:, 0:8]
            f8v = mv8[:, 8:16]
            e8v = mv8[:, 16:24]
            p8v = mv8[:, 24:32]
            d8v = mi8[:, 0:8]
            ix8 = mi8[:, 8:16]
            s1v = mrow[:, 1:3]
            lg3 = lg[:].rearrange("p (c w) -> p c w", c=NH)
            lu3 = lu[:].rearrange("p (c w) -> p c w", c=NH)
            pstE3 = pstE[:].rearrange("p (b c w) -> p b c w", b=4, c=8)

            @blk.sync
            def _(sync):
                for i in range(NI):
                    s = i % 2
                    tt, ph = divmod(i, 14)
                    if tt < NT and ph == 0 and _MIX != "off":
                        sync.wait_ge(mtk.h, tt)
                        sync.dma_start(out=crb[:],
                                       in_=corr_d[tt * 128:(tt + 1) * 128, :]
                                       ).then_inc(mcl.h, mcl.bump(16))
                        if _MIX == "full":
                            if tt >= 1:
                                sync.wait_ge(mack.h, 8 * (tt - 1) + 1)
                            sync.dma_start(
                                out=xsd[:],
                                in_=mix_d[tt * 128:(tt + 1) * 128, :]
                            ).then_inc(mxl.h, mxl.bump(16))
                    if tt < NT and ph == 11 and _MIX == "full":
                        sync.wait_ge(mack.h, 8 * (tt + 1))
                        sync.dma_start(out=mix_d[tt * 128:(tt + 1) * 128, :],
                                       in_=macc[:]).then_inc(mwr.h,
                                                             mwr.bump(16))
                    if i >= 2:
                        sync.wait_ge(edqv.h, i - 1)
                        sync.wait_ge(edqa.h, i - 1)
                    sync.dma_start(out=wg2[:, s * D:(s + 1) * D],
                                   in_=gwT_d[i * 128:(i + 1) * 128, :]
                                   ).then_inc(eldp[s].h, eldp[s].bump(16))
                    sync.dma_start(out=wu2[:, s * D:(s + 1) * D],
                                   in_=uwT_d[i * 128:(i + 1) * 128, :]
                                   ).then_inc(eldp[s].h, eldp[s].bump(16))
                    sync.dma_start(out=guv2[:, s * 128:(s + 1) * 128],
                                   in_=guv_d[i * 128:(i + 1) * 128, :]
                                   ).then_inc(eldp[s].h, eldp[s].bump(16))
                if _MIX == "full":
                    sync.wait_ge(mwr.h, 16 * NT)

            @blk.vector
            def _(vector):
                def edge(inst):
                    inst.then_inc(vch.h, vch.bump(1))
                    vector.wait_ge(vch.h, vch.v)

                for i in range(NI):
                    s = i % 2
                    tt, ph = divmod(i, 14)
                    if tt < NT and _MIX != "off":
                        if ph == 1:
                            vector.wait_ge(mcl.h, 16 * (tt + 1))
                            edge(vector.max(out=m8v, in_=crb[:]))
                            edge(vector.max_index(out=d8v, in_max=m8v,
                                                  in_values=crb[:]))
                            vector.nop().then_inc(mtk.h, mtk.bump(1))
                            if _MIX != "topk":
                                vector.wait_ge(mgi.h, tt + 1)
                                if tt >= 1:
                                    vector.wait_ge(mga.h, 16 * 8 * tt)
                                vector.tensor_scalar(ix8, d8v, mrow[:, 0:1],
                                                     None, op0=ALU.add
                                                     ).then_inc(mid.h,
                                                                mid.bump(1))
                            edge(vector.tensor_scalar(f8v, m8v, m8v[:, 0:1],
                                                      None, op0=ALU.subtract))
                            vector.nop().then_inc(mex.h, mex.bump(1))
                        elif ph == 2:
                            vector.wait_ge(msm.h, tt + 1)
                            edge(vector.reduce_sum(s1v[:, 0:1], e8v, axis=AX.X))
                            edge(vector.reciprocal(s1v[:, 1:2], s1v[:, 0:1]))
                            edge(vector.tensor_scalar(p8v, e8v, s1v[:, 1:2],
                                                      None, op0=ALU.mult))
                        elif 3 <= ph <= 10 and _MIX in ("full", "acc"):
                            k = ph - 3
                            vector.wait_ge(mga.h, 16 * (8 * tt + k + 1))
                            if k == 0:
                                if _MIX == "full":
                                    if tt >= 1:
                                        vector.wait_ge(mwr.h, 16 * tt)
                                    vector.wait_ge(mxl.h, 16 * (tt + 1))
                                    edge(vector.scalar_tensor_tensor(
                                        out=macc[:], in0=gb1[:],
                                        scalar=p8v[:, 0:1], in1=xsd[:],
                                        op0=ALU.mult, op1=ALU.add))
                                else:
                                    edge(vector.tensor_scalar(
                                        macc[:], gb1[:], p8v[:, 0:1], None,
                                        op0=ALU.mult))
                            else:
                                edge(vector.scalar_tensor_tensor(
                                    out=macc[:], in0=gb1[:],
                                    scalar=p8v[:, k:k + 1], in1=macc[:],
                                    op0=ALU.mult, op1=ALU.add))
                            vector.nop().then_inc(mack.h, mack.bump(1))
                    sv = s * 32
                    vector.wait_ge(eldp[s].h, 48 * (i // 2 + 1))
                    gsv = guv2[:, s * 128:s * 128 + 32].bitcast(F32)
                    gzv = guv2[:, s * 128 + 32:s * 128 + 64]
                    usv = guv2[:, s * 128 + 64:s * 128 + 96].bitcast(F32)
                    uzv = guv2[:, s * 128 + 96:s * 128 + 128]
                    edge(vector.tensor_tensor(zsg[:, sv:sv + 32], gzv, gsv,
                                              op=ALU.mult))
                    edge(vector.tensor_tensor(zsu[:, sv:sv + 32], uzv, usv,
                                              op=ALU.mult))
                    vector.tensor_scalar(nzsu[:, sv:sv + 32], zsu[:, sv:sv + 32],
                                         -1.0, None,
                                         op0=ALU.mult).then_inc(ezs.h, ezs.bump(1))
                    if i > 0:
                        vector.wait_ge(etr.h, 8 * i - 4)
                    for c in range(NH):
                        ins = vector.tensor_scalar(
                            dqg[:, c * 128:(c + 1) * 128],
                            wg2[:, s * D + c * 128:s * D + (c + 1) * 128],
                            guv2[:, s * 128 + c:s * 128 + c + 1].bitcast(F32),
                            zsg[:, sv + c:sv + c + 1],
                            op0=ALU.mult, op1=ALU.subtract)
                    ins.then_inc(edqv.h, edqv.bump(1))
                    for r in range(4):
                        vector.wait_ge(etr.h, 8 * i + r + 1)
                        if r == 0 and i > 0:
                            vector.wait_ge(emm.h, 4 * i - 2)
                        vector.tensor_copy(lg3[:, r * 8:(r + 1) * 8, :],
                                           pstE3[:, r, :, :]
                                           ).then_inc(eevg.h, eevg.bump(1))
                    if i > 0:
                        for ts in range(2):
                            vector.wait_ge(esil.h, 2 * (i - 1) + ts + 1)
                            vector.wait_ge(emm.h, 4 * (i - 1) + 2 + ts + 1)
                            if i >= 3 and ts == 0:
                                vector.wait_ge(ehwp[(i - 3) % 2].h,
                                               16 * ((i - 3) // 2 + 1))
                            vector.tensor_tensor(
                                hb[:, ((i - 1) % 2) * 1024 + ts * 512:
                                   ((i - 1) % 2) * 1024 + (ts + 1) * 512],
                                sg[:, ts * 512:(ts + 1) * 512],
                                psu[:, ts * 512:(ts + 1) * 512],
                                op=ALU.mult).then_inc(ehm.h, ehm.bump(1))
                i = NI - 1
                for ts in range(2):
                    vector.wait_ge(esil.h, 2 * i + ts + 1)
                    vector.wait_ge(emm.h, 4 * i + 2 + ts + 1)
                    vector.tensor_tensor(
                        hb[:, (i % 2) * 1024 + ts * 512:
                           (i % 2) * 1024 + (ts + 1) * 512],
                        sg[:, ts * 512:(ts + 1) * 512],
                        psu[:, ts * 512:(ts + 1) * 512],
                        op=ALU.mult).then_inc(ehm.h, ehm.bump(1))

            @blk.scalar
            def _(scalar):
                def dequ(i):
                    s = i % 2
                    scalar.wait_ge(ezs.h, i + 1)
                    if i > 0:
                        scalar.wait_ge(etr.h, 8 * i)
                    for c in range(NH):
                        ins = scalar.activation(
                            dqu[:, c * 128:(c + 1) * 128],
                            wu2[:, s * D + c * 128:s * D + (c + 1) * 128],
                            AF.Identity,
                            scale=guv2[:, s * 128 + 64 + c:s * 128 + 64 + c + 1
                                       ].bitcast(F32),
                            bias=nzsu[:, s * 32 + c:s * 32 + c + 1])
                    ins.then_inc(edqa.h, edqa.bump(1))

                dequ(0)
                for i in range(NI):
                    tt, ph = divmod(i, 14)
                    if tt < NT and ph == 1 and _MIX != "off":
                        scalar.wait_ge(mex.h, tt + 1)
                        scalar.activation(e8v, f8v, AF.Exp
                                          ).then_inc(msm.h, msm.bump(1))
                    for ru in range(4):
                        scalar.wait_ge(etr.h, 8 * i + 4 + ru + 1)
                        if ru == 0 and i > 0:
                            scalar.wait_ge(emm.h, 4 * i)
                        scalar.activation(lu3[:, ru * 8:(ru + 1) * 8, :],
                                          pstE3[:, ru, :, :],
                                          AF.Identity).then_inc(eevu.h,
                                                                eevu.bump(1))
                    if i > 0:
                        scalar.wait_ge(ehm.h, 2 * i)
                        scalar.dma_start(
                            out=h_d[(i - 1) * 128:i * 128, :],
                            in_=hb[:, ((i - 1) % 2) * 1024:((i - 1) % 2 + 1) * 1024]
                            ).then_inc(ehwp[(i - 1) % 2].h,
                                       ehwp[(i - 1) % 2].bump(16))
                    if i < NI - 1:
                        dequ(i + 1)
                    for ts in range(2):
                        scalar.wait_ge(emm.h, 4 * i + ts + 1)
                        if i > 0 and ts == 0:
                            scalar.wait_ge(ehm.h, 2 * i)
                        scalar.activation(sg[:, ts * 512:(ts + 1) * 512],
                                          psg[:, ts * 512:(ts + 1) * 512],
                                          AF.Silu).then_inc(esil.h, esil.bump(1))
                scalar.wait_ge(ehm.h, 2 * NI)
                scalar.dma_start(
                    out=h_d[(NI - 1) * 128:NI * 128, :],
                    in_=hb[:, ((NI - 1) % 2) * 1024:((NI - 1) % 2 + 1) * 1024]
                    ).then_inc(ehwp[(NI - 1) % 2].h,
                               ehwp[(NI - 1) % 2].bump(16))
                scalar.wait_ge(ehwp[0].h, 16 * (NI // 2))
                scalar.wait_ge(ehwp[1].h, 16 * (NI // 2))

            @blk.gpsimd
            def _(gpsimd):
                if _MIX in ("off", "topk"):
                    return
                xdup_flat = xdup_d.rearrange("r (c one) -> (r c) one", one=1)
                for tt in range(NT):
                    gpsimd.wait_ge(mid.h, tt)
                    gpsimd.iota(mrow[:, 0:1], [[0, 1]], base=tt * 128 * 8192,
                                channel_multiplier=8192,
                                allow_small_or_imprecise_dtypes=True
                                ).then_inc(mgi.h, mgi.bump(1))
                    for k in range(8):
                        gk = 8 * tt + k
                        gpsimd.wait_ge(mid.h, tt + 1)
                        if gk >= 1 and _MIX in ("full", "acc"):
                            gpsimd.wait_ge(mack.h, gk)
                        gpsimd.indirect_dma_start(
                            out=gb1[:], out_offset=None,
                            in_=xdup_flat,
                            in_offset=bass.IndirectOffsetOnAxis(
                                ap=ix8[:, k:k + 1], axis=0),
                        ).then_inc(mga.h, mga.bump(16))


            @blk.tensor
            def _(tensor):
                for i in range(NI):
                    for r in range(8):
                        if r == 0:
                            tensor.wait_ge(edqv.h, i + 1)
                        if r == 4:
                            tensor.wait_ge(edqa.h, i + 1)
                        if r < 4:
                            if i > 0:
                                tensor.wait_ge(eevu.h, 4 * (i - 1) + r + 1)
                        else:
                            tensor.wait_ge(eevg.h, 4 * i + (r - 4) + 1)
                        src_t = dqg if r < 4 else dqu
                        rb = r % 4
                        for k in range(8):
                            c = rb * 8 + k
                            t = tensor.transpose(pstE3[:, r % 4, k, :],
                                                 src_t[:, c * 128:(c + 1) * 128],
                                                 idb[:])
                            if k == 7:
                                t.then_inc(etr.h, etr.bump(1))
                    tensor.wait_ge(eevg.h, 4 * i + 4)
                    if i > 0:
                        tensor.wait_ge(esil.h, 2 * i)
                    for ts in range(2):
                        for c in range(NH):
                            mm = tensor.matmul(psg[:, ts * 512:(ts + 1) * 512],
                                               lg3[:, c, :],
                                               xnt3[:, c, ts * 512:(ts + 1) * 512],
                                               start=(c == 0), stop=(c == NH - 1))
                        mm.then_inc(emm.h, emm.bump(1))
                    tensor.wait_ge(eevu.h, 4 * i + 4)
                    if i > 0:
                        tensor.wait_ge(ehm.h, 2 * i)
                    for ts in range(2):
                        for c in range(NH):
                            mm = tensor.matmul(psu[:, ts * 512:(ts + 1) * 512],
                                               lu3[:, c, :],
                                               xnt3[:, c, ts * 512:(ts + 1) * 512],
                                               start=(c == 0), stop=(c == NH - 1))
                        mm.then_inc(emm.h, emm.bump(1))

        ectx.close()

        xnt_cm.__exit__(None, None, None)
        if _SUB == "E":
            with nc.Block() as blk:
                @blk.sync
                def _(sync):
                    sync.wait_ge(sems["ehw0"].h, sems["ehw0"].v)
                    sync.wait_ge(sems["ehw1"].h, sems["ehw1"].v)
            ctx.close()
            return nc
        fdlp = [sem("fdl0"), sem("fdl1")]
        fw1p = [sem("fw10"), sem("fw11")]
        fzs, fqv, fqa = sem("fzs"), sem("fqv"), sem("fqa")
        ftr, fevv, feva = sem("ftr"), sem("fevv"), sem("feva")
        fwrp = [sem("fwr0"), sem("fwr1")]
        with (
            nc.sbuf_tensor("dwq", [128, 4 * 7168], I32) as dwq,
            nc.sbuf_tensor("dqd", [128, 4 * 7168], BF16) as dqd,
            nc.sbuf_tensor("dlb", [128, 2 * 7168], BF16) as dlb,
            nc.sbuf_tensor("dsv", [128, 2 * NI], F32) as dsv,
            nc.sbuf_tensor("dzv", [128, 2 * NI], I32) as dzv,
            nc.sbuf_tensor("zsd", [128, 2 * NI], F32) as zsd,
            nc.sbuf_tensor("nzsd", [128, 2 * NI], F32) as nzsd,
            nc.psum_tensor("pstF", [128, 4 * 1024], BF16) as pstF,
            nc.Block() as blk,
        ):
            pstF3 = pstF[:].rearrange("p (b c w) -> p b c w", b=4, c=8)
            dwq4 = dwq[:].rearrange("p (s w) -> p s w", s=4)
            dqd4 = dqd[:].rearrange("p (s w) -> p s w", s=4)
            dlb2 = dlb[:].rearrange("p (s w) -> p s w", s=2)

            @blk.sync
            def _(sync):
                for fi in range(32):
                    s = fi % 2
                    if fi >= 2:
                        sync.wait_ge(fqv.h, fi - 1)
                        sync.wait_ge(fqa.h, fi - 1)
                    sync.dma_start(out=dsv[:, s * NI:(s + 1) * NI],
                                   in_=dsT_d[fi * 128:(fi + 1) * 128, :]
                                   ).then_inc(fdlp[s].h, fdlp[s].bump(16))
                    sync.dma_start(out=dzv[:, s * NI:(s + 1) * NI],
                                   in_=dzT_d[fi * 128:(fi + 1) * 128, :]
                                   ).then_inc(fdlp[s].h, fdlp[s].bump(16))
                    for hf in range(2):
                        if fi >= 2:
                            if hf == 0:
                                sync.wait_ge(fqv.h, fi - 1)
                            else:
                                sync.wait_ge(fqa.h, fi - 1)
                        sync.dma_start(
                            out=dwq4[:, s * 2 + hf, :],
                            in_=dwT_d[fi * 128:(fi + 1) * 128,
                                      hf * 7168:(hf + 1) * 7168]
                            ).then_inc((fdlp if hf == 0 else fw1p)[s].h,
                                       (fdlp if hf == 0 else fw1p)[s].bump(16))
                    if fi >= 1:
                        sync.wait_ge(fevv.h, 7 * fi)
                        sync.dma_start(out=dl_d[fi - 1][:, 0:7168],
                                       in_=dlb2[:, 0, :]
                                       ).then_inc(fwrp[0].h, fwrp[0].bump(16))
                        sync.wait_ge(feva.h, 7 * fi)
                        sync.dma_start(out=dl_d[fi - 1][:, 7168:2 * 7168],
                                       in_=dlb2[:, 1, :]
                                       ).then_inc(fwrp[1].h, fwrp[1].bump(16))
                sync.wait_ge(fevv.h, 7 * 32)
                sync.dma_start(out=dl_d[31][:, 0:7168], in_=dlb2[:, 0, :]
                               ).then_inc(fwrp[0].h, fwrp[0].bump(16))
                sync.wait_ge(feva.h, 7 * 32)
                sync.dma_start(out=dl_d[31][:, 7168:2 * 7168], in_=dlb2[:, 1, :]
                               ).then_inc(fwrp[1].h, fwrp[1].bump(16))
                sync.wait_ge(fwrp[0].h, 16 * 32)
                sync.wait_ge(fwrp[1].h, 16 * 32)

            @blk.vector
            def _(vector):
                def edge(inst):
                    inst.then_inc(vch.h, vch.bump(1))
                    vector.wait_ge(vch.h, vch.v)

                for fi in range(32):
                    s = fi % 2
                    vector.wait_ge(fdlp[s].h, 48 * (fi // 2 + 1))
                    edge(vector.tensor_tensor(zsd[:, s * NI:(s + 1) * NI],
                                              dzv[:, s * NI:(s + 1) * NI],
                                              dsv[:, s * NI:(s + 1) * NI],
                                              op=ALU.mult))
                    vector.tensor_scalar(nzsd[:, s * NI:(s + 1) * NI],
                                         zsd[:, s * NI:(s + 1) * NI], -1.0, None,
                                         op0=ALU.mult).then_inc(fzs.h, fzs.bump(1))
                    if fi >= 2:
                        vector.wait_ge(ftr.h, 14 * (fi - 2) + 7)
                    for j in range(56):
                        ins = vector.tensor_scalar(
                            dqd4[:, s * 2, j * 128:(j + 1) * 128],
                            dwq4[:, s * 2, j * 128:(j + 1) * 128],
                            dsv[:, s * NI + j:s * NI + j + 1],
                            zsd[:, s * NI + j:s * NI + j + 1],
                            op0=ALU.mult, op1=ALU.subtract)
                    ins.then_inc(fqv.h, fqv.bump(1))
                    for r in range(7):
                        gr = 14 * fi + r
                        vector.wait_ge(ftr.h, gr + 1)
                        if r == 0 and fi >= 1:
                            vector.wait_ge(fwrp[0].h, 16 * fi)
                        vector.tensor_copy(
                            dlb2[:, 0, r * 1024:(r + 1) * 1024]
                            .rearrange("p (c w) -> p c w", c=8),
                            pstF3[:, gr % 4, :, :]
                            ).then_inc(fevv.h, fevv.bump(1))

            @blk.scalar
            def _(scalar):
                for fi in range(32):
                    s = fi % 2
                    scalar.wait_ge(fzs.h, fi + 1)
                    scalar.wait_ge(fw1p[s].h, 16 * (fi // 2 + 1))
                    if fi >= 2:
                        scalar.wait_ge(ftr.h, 14 * (fi - 1))
                    for j in range(56):
                        ins = scalar.activation(
                            dqd4[:, s * 2 + 1, j * 128:(j + 1) * 128],
                            dwq4[:, s * 2 + 1, j * 128:(j + 1) * 128],
                            AF.Identity,
                            scale=dsv[:, s * NI + 56 + j:s * NI + 56 + j + 1],
                            bias=nzsd[:, s * NI + 56 + j:s * NI + 56 + j + 1])
                    ins.then_inc(fqa.h, fqa.bump(1))
                    for r in range(7, 14):
                        gr = 14 * fi + r
                        scalar.wait_ge(ftr.h, gr + 1)
                        if r == 7 and fi >= 1:
                            scalar.wait_ge(fwrp[1].h, 16 * fi)
                        scalar.activation(dlb2[:, 1, (r - 7) * 1024:(r - 6) * 1024]
                                          .rearrange("p (c w) -> p c w", c=8),
                                          pstF3[:, gr % 4, :, :],
                                          AF.Identity).then_inc(feva.h,
                                                                feva.bump(1))

            @blk.tensor
            def _(tensor):
                for fi in range(32):
                    s = fi % 2
                    for r in range(14):
                        gr = 14 * fi + r
                        if r == 0:
                            tensor.wait_ge(fqv.h, fi + 1)
                        if r == 7:
                            tensor.wait_ge(fqa.h, fi + 1)
                        if gr >= 4:
                            fi2, r2 = divmod(gr - 4, 14)
                            if r2 < 7:
                                tensor.wait_ge(fevv.h, 7 * fi2 + r2 + 1)
                            else:
                                tensor.wait_ge(feva.h, 7 * fi2 + (r2 - 7) + 1)
                        hf = 0 if r < 7 else 1
                        for k in range(8):
                            j = (r - 7 * hf) * 8 + k
                            t = tensor.transpose(
                                pstF3[:, gr % 4, k, :],
                                dqd4[:, s * 2 + hf, j * 128:(j + 1) * 128],
                                idb[:])
                            if k == 7:
                                t.then_inc(ftr.h, ftr.bump(1))

        NS = 16
        NB = 14
        fllp = [sem("fll0"), sem("fll1"), sem("fll2"), sem("fll3")]
        fhlp = [sem("fhl0"), sem("fhl1"), sem("fhl2"), sem("fhl3")]
        fmm, foe = sem("fmm"), sem("foe")
        fot, foa = sem("fot"), sem("foa")
        fmip = [sem("fmi0"), sem("fmi1")]
        fowp = [sem("fow0"), sem("fow1")]
        B = {"fll": [2 * s for s in range(NS + 2)],
             "fhl": [NB * s for s in range(NS + 2)],
             "fmm": [NB * s for s in range(NS + 2)],
             "foe": [4 * s for s in range(NS + 2)],
             "fot": [8 * s for s in range(NS + 2)],
             "fmi": [8 * s for s in range(NS + 2)],
             "foa": [8 * s for s in range(NS + 2)],
             "fow": [8 * s for s in range(NS + 2)]}
        with (
            nc.sbuf_tensor("DLp", [128, 2 * 2 * DI], BF16) as DLp,
            nc.sbuf_tensor("hhp", [128, 4 * 8192], BF16) as hh,
            nc.sbuf_tensor("o2p", [128, 2 * 2048], F32) as o2sb,
            nc.sbuf_tensor("mxp", [128, 2 * 256], BF16) as mxsb,
            nc.sbuf_tensor("oup", [128, 2 * 256], F32) as outsb,
            nc.psum_tensor("psFp", [128, 4096], F32) as psF,
            nc.Block() as blk,
        ):
            DL4 = DLp[:].rearrange("p (sl ft w) -> p sl ft w", sl=2, ft=2)
            hh3 = hh[:].rearrange("p (b c t) -> p b c t", b=4, c=8)
            o22 = o2sb[:].rearrange("p (sl w) -> p sl w", sl=2)

            @blk.gpsimd
            def _(gpsimd):
                for s in range(NS):
                    for tb in range(8):
                        if B["foa"][s] + tb - 1 > 0:
                            gpsimd.wait_ge(foa.h, B["foa"][s] + tb - 1)
                        gpsimd.dma_start(
                            out=mxsb[:, (tb % 2) * 256:(tb % 2 + 1) * 256],
                            in_=mix_d[tb * 128:(tb + 1) * 128,
                                      s * 256:(s + 1) * 256]
                        ).then_inc(fmip[tb % 2].h, fmip[tb % 2].bump(16))
                        gpsimd.wait_ge(foa.h, B["foa"][s] + tb + 1)
                        gpsimd.dma_start(
                            out=out_d[tb * 128:(tb + 1) * 128,
                                      s * 256:(s + 1) * 256],
                            in_=outsb[:, (tb % 2) * 256:(tb % 2 + 1) * 256]
                        ).then_inc(fowp[tb % 2].h, fowp[tb % 2].bump(16))

            @blk.sync
            def _(sync):
                for ft in range(2):
                    sync.dma_start(out=DL4[:, 0, ft, :], in_=dl_d[ft]
                                   ).then_inc(fllp[ft].h, fllp[ft].bump(16))
                for s in range(NS):
                    for bi in range(NB):
                        gb = s * NB + bi
                        if gb >= 4:
                            s2, b2 = divmod(gb - 4, NB)
                            sync.wait_ge(fmm.h, B["fmm"][s2] + b2 + 1)
                        sync.dma_start(
                            out=hh3[:, gb % 4, :, :],
                            in_=h_d[bi * 1024:(bi + 1) * 1024, :]
                            .rearrange("(c p) t -> p c t", p=128)
                        ).then_inc(fhlp[gb % 4].h, fhlp[gb % 4].bump(16))
                        if bi == 1 and s + 1 < NS:
                            if s >= 1:
                                sync.wait_ge(fmm.h, B["fmm"][s])
                            for ft in range(2):
                                q2 = ((s + 1) % 2) * 2 + ft
                                sync.dma_start(out=DL4[:, (s + 1) % 2, ft, :],
                                               in_=dl_d[2 * (s + 1) + ft]
                                               ).then_inc(fllp[q2].h,
                                                          fllp[q2].bump(16))


            @blk.scalar
            def _(scalar):
                for s in range(NS):
                    scalar.wait_ge(fmm.h, B["fmm"][s] + NB)
                    if s >= 2:
                        scalar.wait_ge(fot.h, B["fot"][s - 2] + 8)
                    sb = (s % 2) * 2048
                    for b in (0, 2, 1, 3):
                        scalar.activation(o22[:, s % 2, b * 512:(b + 1) * 512],
                                          psF[:, sb + b * 512:sb + (b + 1) * 512],
                                          AF.Identity).then_inc(foe.h, foe.bump(1))

            @blk.tensor
            def _(tensor):
                def mms(s, tail_fn=None):
                    sb = (s % 2) * 2048
                    if s >= 2:
                        tensor.wait_ge(foa.h, B["foa"][s - 2] + 8)
                    tensor.wait_ge(fhlp[(s * NB) % 4].h,
                                   16 * ((s * NB) // 4 + 1))
                    for ft in range(2):
                        tensor.wait_ge(fllp[(s % 2) * 2 + ft].h,
                                       16 * (s // 2 + 1))
                        for ci in range(8):
                            for ts in range(2):
                                b = ft * 2 + ts
                                tensor.matmul(
                                    psF[:, sb + b * 512:sb + (b + 1) * 512],
                                    DL4[:, s % 2, ft, ci * 128:(ci + 1) * 128],
                                    hh3[:, (s * NB) % 4, ci,
                                        ts * 512:(ts + 1) * 512],
                                    start=(ci == 0), stop=False)
                    for ci in range(8, NI):
                        bi = ci // 8
                        gb = s * NB + bi
                        tensor.wait_ge(fhlp[gb % 4].h, 16 * (gb // 4 + 1))
                        for ts in range(2):
                            for ft in range(2):
                                b = ft * 2 + ts
                                mm = tensor.matmul(
                                    psF[:, sb + b * 512:sb + (b + 1) * 512],
                                    DL4[:, s % 2, ft, ci * 128:(ci + 1) * 128],
                                    hh3[:, gb % 4, ci % 8,
                                        ts * 512:(ts + 1) * 512],
                                    start=False, stop=(ci == NI - 1))
                        if ci % 8 == 7:
                            inc = 2 if ci == 15 else 1
                            mm.then_inc(fmm.h, fmm.bump(inc))
                        if ci == 23 and tail_fn is not None:
                            tail_fn()

                def tails(s):
                    sb = (s % 2) * 2048
                    for tb in range(8):
                        ts = tb // 4
                        tensor.wait_ge(foe.h, B["foe"][s] + 4)
                        for ft in range(2):
                            src = o22[:, s % 2,
                                      (ft * 2 + ts) * 512 + (tb % 4) * 128:
                                      (ft * 2 + ts) * 512 + (tb % 4 + 1) * 128]
                            t = tensor.transpose(
                                psF[:, sb + (tb // 2) * 512 + (tb % 2) * 256
                                    + ft * 128:
                                    sb + (tb // 2) * 512 + (tb % 2) * 256
                                    + (ft + 1) * 128],
                                src, idf_s[:])
                        t.then_inc(fot.h, fot.bump(1))

                mms(0)
                for s in range(1, NS):
                    mms(s, tail_fn=lambda s2=s - 1: tails(s2))
                tails(NS - 1)

            @blk.vector
            def _(vector):
                def edge(inst):
                    inst.then_inc(vch.h, vch.bump(1))
                    vector.wait_ge(vch.h, vch.v)

                for s in range(NS):
                    sb = (s % 2) * 2048
                    for tb in range(8):
                        vector.wait_ge(fot.h, B["fot"][s] + (tb | 1) + 1)
                        vector.wait_ge(fmip[tb % 2].h,
                                       16 * ((8 * s + tb) // 2 + 1))
                        wo = 8 * s + tb - 2
                        if wo >= 0:
                            vector.wait_ge(fowp[wo % 2].h, 16 * (wo // 2 + 1))
                        edge(vector.tensor_tensor(
                            outsb[:, (tb % 2) * 256:(tb % 2 + 1) * 256],
                            psF[:, sb + (tb // 2) * 512 + (tb % 2) * 256:
                                sb + (tb // 2) * 512 + (tb % 2 + 1) * 256],
                            mxsb[:, (tb % 2) * 256:(tb % 2 + 1) * 256],
                            op=ALU.add))
                        vector.nop().then_inc(foa.h, foa.bump(1))

    if stage < 3:
        if stage == 1:
            ppr_cm.__exit__(None, None, None)
        xnt_cm.__exit__(None, None, None)
    with nc.Block() as blk:
        @blk.sync
        def _(sync):
            if stage == 1:
                sync.wait_ge(sems["aout"].h, sems["aout"].v)
            if stage == 2:
                sync.wait_ge(sems["dwr"].h, sems["dwr"].v)
            if stage >= 3 and _SUB != "P":
                sync.wait_ge(sems["fow0"].h, sems["fow0"].v)
                sync.wait_ge(sems["fow1"].h, sems["fow1"].v)
                sync.wait_ge(sems["ehw0"].h, sems["ehw0"].v)
                sync.wait_ge(sems["ehw1"].h, sems["ehw1"].v)
            sync.wait_ge(sems["adup"].h, sems["adup"].v)
            sync.wait_ge(sems["aev"].h, sems["aev"].v)

    ctx.close()
    return nc


def _prep_host(inputs):
    x = np.ascontiguousarray(np.asarray(inputs["L0_attn_x"], dtype=np.float32)
                             ).reshape(8192, D)
    cons = {}
    cons["idb"] = np.eye(128, dtype=np.float32).astype(ml_dtypes.bfloat16)
    cons["idf"] = np.eye(128, dtype=np.float32)
    j = np.arange(D)[:, None].astype(np.float64)
    cfwd = np.empty((8, D, 512), dtype=ml_dtypes.bfloat16)
    for g in range(8):
        f = (g * 256 + np.arange(256))[None, :].astype(np.float64)
        ang = 2.0 * np.pi * j * f / D
        cfwd[g, :, 0:256] = np.cos(ang).astype(ml_dtypes.bfloat16)
        cfwd[g, :, 256:512] = np.sin(ang).astype(ml_dtypes.bfloat16)
    cons["cfwd"] = cfwd
    c2 = np.empty((4, NF2, 1024), dtype=ml_dtypes.bfloat16)
    f = np.arange(NF2)[:, None].astype(np.float64)
    w = np.full((NF2, 1), 2.0 / D); w[0, 0] = 1.0 / D
    for q in range(4):
        lag = (q * 1024 + np.arange(1024))[None, :].astype(np.float64)
        c2[q] = (w * np.cos(2.0 * np.pi * f * lag / D)).astype(ml_dtypes.bfloat16)
    cons["c2"] = c2
    cons["gwT"] = np.ascontiguousarray(np.asarray(inputs["L0_ffn_gate_w"]).T)
    cons["uwT"] = np.ascontiguousarray(np.asarray(inputs["L0_ffn_up_w"]).T)
    cons["dwT"] = np.ascontiguousarray(np.asarray(inputs["L0_ffn_down_w"]).T)
    gsT = np.ascontiguousarray(np.asarray(inputs["L0_ffn_gate_s"]).T)
    usT = np.ascontiguousarray(np.asarray(inputs["L0_ffn_up_s"]).T)
    gzT = np.ascontiguousarray(np.asarray(inputs["L0_ffn_gate_z"]).T)
    uzT = np.ascontiguousarray(np.asarray(inputs["L0_ffn_up_z"]).T)
    cons["guv"] = np.ascontiguousarray(np.concatenate(
        [gsT.view(np.int32), gzT, usT.view(np.int32), uzT], axis=1))
    cons["dsT"] = np.ascontiguousarray(np.asarray(inputs["L0_ffn_down_s"]).T)
    cons["dzT"] = np.ascontiguousarray(np.asarray(inputs["L0_ffn_down_z"]).T)
    return x, cons


def kernel(**inputs):
    x, cons = _prep_host(inputs)
    nc = build_program(_STAGE)
    names = {"idb", "idf", "cfwd", "c2", "gwT", "uwT", "dwT",
             "guv", "dsT", "dzT"}
    if _STAGE < 3:
        for k in ["idf", "gwT", "uwT", "dwT", "guv", "dsT", "dzT"]:
            names.discard(k)
    if _STAGE < 2:
        for k in ["cfwd", "c2"]:
            names.discard(k)
    in_maps = []
    for c in range(NCORES):
        m = {"x": np.ascontiguousarray(x[c * R:(c + 1) * R])}
        for k in names:
            m[k] = cons[k]
        in_maps.append(m)
    res = run_bass_kernel_spmd(nc, in_maps, core_ids=list(range(NCORES)))
    out = np.concatenate([res.results[c]["out"] for c in range(NCORES)], axis=0)
    return out.reshape(4, 2048, D).astype(np.float32)


# version 15 (final)
# speedup vs baseline: 1.0157x; 1.0157x over previous
import sys
import numpy as np
import ml_dtypes
from contextlib import ExitStack

sys.path.insert(0, "/opt/trn_rl_repo")
import concourse.bass as bass
import concourse.mybir as mybir
from concourse.bass_utils import run_bass_kernel_spmd

F32 = mybir.dt.float32
BF16 = mybir.dt.bfloat16
I32 = mybir.dt.int32
U32 = mybir.dt.uint32
AF = mybir.ActivationFunctionType
ALU = mybir.AluOpType
AX = mybir.AxisListType

NCORES = 8
R = 1024
D = 4096
DI = 14336
NT = R // 128
NH = D // 128
NI = DI // 128
NF2 = 2048
EPS = 1e-5

import os
_STAGE = 3
_SUB = os.environ.get("K_SUB", "")
_NQ = int(os.environ.get("K_NQ", "8"))
_NFT = int(os.environ.get("K_NFT", "4"))
_NOTR = os.environ.get("K_NOTR", "")
_MIX = os.environ.get("K_MIX", "full")
_DIAG = False


class Sem:
    def __init__(self, ctx, nc, name):
        self.h = ctx.enter_context(nc.semaphore(name))
        self.v = 0

    def inc(self, n=1):
        self.v += n
        return self.v

    def bump(self, n=1):
        self.v += n
        return n


def build_program(stage):
    nc = bass.Bass("TRN2", target_bir_lowering=False)
    ctx = ExitStack()

    x_d = nc.dram_tensor("x", [R, D], F32, kind="ExternalInput").ap()
    idb_d = nc.dram_tensor("idb", [128, 128], BF16, kind="ExternalInput").ap()
    out_d = nc.dram_tensor("out", [R, D], F32, kind="ExternalOutput").ap()
    xdup_d = nc.dram_tensor("xdup", [R, 2 * D], BF16, kind="Internal").ap()
    mix_d = nc.dram_tensor("mix", [R, D], BF16, kind="Internal").ap()
    corr_d = nc.dram_tensor("corrd", [R, D], BF16, kind="Internal").ap()
    xnt_d = nc.dram_tensor("xntd", [128, NH * R], BF16, kind="Internal").ap()
    if stage >= 2:
        cfwd_d = nc.dram_tensor("cfwd", [8, D, 512], BF16, kind="ExternalInput").ap()
        c2_d = nc.dram_tensor("c2", [4, NF2, 1024], BF16, kind="ExternalInput").ap()
    if stage >= 3:
        idf_d = nc.dram_tensor("idf", [128, 128], F32, kind="ExternalInput").ap()
        gwT_d = nc.dram_tensor("gwT", [DI, D], I32, kind="ExternalInput").ap()
        uwT_d = nc.dram_tensor("uwT", [DI, D], I32, kind="ExternalInput").ap()
        dwT_d = nc.dram_tensor("dwT", [D, DI], I32, kind="ExternalInput").ap()
        guv_d = nc.dram_tensor("guv", [DI, 128], I32, kind="ExternalInput").ap()
        dsT_d = nc.dram_tensor("dsT", [D, NI], F32, kind="ExternalInput").ap()
        dzT_d = nc.dram_tensor("dzT", [D, NI], I32, kind="ExternalInput").ap()
        h_d = nc.dram_tensor("hd", [DI, R], BF16, kind="Internal").ap()
        dl_d = nc.dram_tensor("dld", [32, 128, DI], BF16, kind="Internal").ap()

    idb = ctx.enter_context(nc.sbuf_tensor("idb_s", [128, 128], BF16))
    pnyq = ctx.enter_context(nc.sbuf_tensor("pnyq", [128, NT], F32))
    epsc = ctx.enter_context(nc.sbuf_tensor("epsc", [128, 1], F32))
    idf_s = ctx.enter_context(nc.sbuf_tensor("idfp", [128, 128], F32))
    xnt_cm = nc.sbuf_tensor("xnt", [128, NH * R], BF16)
    xnt = xnt_cm.__enter__()
    xnt3 = xnt[:].rearrange("p (c t) -> p c t", c=NH)
    ppr_cm = nc.sbuf_tensor("ppr", [128, 16 * 1024], BF16)
    ppr = ppr_cm.__enter__()
    ppr3 = ppr[:].rearrange("p (fc t) -> p fc t", fc=16)

    sems = {}

    def sem(name):
        if name not in sems:
            sems[name] = Sem(ctx, nc, "s_" + name)
        return sems[name]

    ldx = sem("ldx")
    vvar = sem("vvar")
    srst = sem("srst")
    axnb = sem("axnb")
    adup = sem("adup")
    atr = sem("atr")
    aev = sem("aev")
    aout = sem("aout")

    with (
        nc.sbuf_tensor("xa", [128, 2 * D], F32) as xa,
        nc.sbuf_tensor("xnf", [128, D], F32) as xnf,
        nc.sbuf_tensor("xnb", [128, D], BF16) as xnb,
        nc.sbuf_tensor("sta", [128, 16], F32) as sta,
        nc.sbuf_tensor("xb", [128, D], BF16) as xb,
        nc.sbuf_tensor("cfb", [128, 2 * 16 * 512], BF16) as cfb,
        nc.sbuf_tensor("sqb", [128, 4 * 512], F32) as sqb,
        nc.psum_tensor("pstA", [128, 2 * 1024], BF16) as pstA,
        nc.psum_tensor("psB4", [128, 2048], F32) as psB4,
        nc.Block() as blk,
    ):
        cfb4 = cfb[:].rearrange("p (s c w) -> p s c w", s=2, c=16)
        bldp = [sem("bld0"), sem("bld1")]
        bmm, bsq, bev = sem("bmm"), sem("bsq"), sem("bev")
        pstA3 = pstA[:].rearrange("p (b c t) -> p b c t", b=2, c=8)

        amw = sem("amw")
        axb = sem("axb")

        @blk.sync
        def _(sync):
            sync.dma_start(out=idb[:], in_=idb_d).then_inc(ldx.h, ldx.bump(16))
            if stage >= 3:
                sync.dma_start(out=idf_s[:], in_=idf_d).then_inc(
                    ldx.h, ldx.bump(16))
            for tt in range(NT):
                sync.dma_start(out=xa[:, (tt % 2) * D:(tt % 2 + 1) * D],
                               in_=x_d[tt * 128:(tt + 1) * 128, :]
                               ).then_inc(ldx.h, ldx.bump(16))
                sync.wait_ge(axnb.h, tt + 1)
                sync.dma_start(out=xdup_d[tt * 128:(tt + 1) * 128, 0:D],
                               in_=xnb[:]).then_inc(adup.h, adup.bump(16))
                sync.dma_start(out=xdup_d[tt * 128:(tt + 1) * 128, D:2 * D],
                               in_=xnb[:]).then_inc(adup.h, adup.bump(16))
                if stage == 1:
                    sync.dma_start(out=out_d[tt * 128:(tt + 1) * 128, :],
                                   in_=xnf[:]).then_inc(aout.h, aout.bump(16))
                if stage >= 3:
                    sync.wait_ge(axb.h, tt + 1)
                    sync.dma_start(out=mix_d[tt * 128:(tt + 1) * 128, :],
                                   in_=xb[:]).then_inc(amw.h, amw.bump(16))

        vch = sem("vch")
        sch = sem("sch")

        @blk.scalar
        def _(scalar):
            for tt in range(NT):
                scalar.wait_ge(vvar.h, tt + 1)
                scalar.activation(sta[:, 2:3], sta[:, 1:2], AF.Ln,
                                  bias=epsc[:, 0:1]).then_inc(sch.h, sch.bump(1))
                scalar.wait_ge(sch.h, sch.v)
                scalar.activation(sta[:, 3:4], sta[:, 2:3], AF.Exp,
                                  scale=-0.5).then_inc(srst.h, srst.bump(1))

        ldx0 = 32 if stage >= 3 else 16

        @blk.vector
        def _(vector):
            def edge(inst):
                inst.then_inc(vch.h, vch.bump(1))
                vector.wait_ge(vch.h, vch.v)

            vector.memset(epsc[:], EPS)
            for tt in range(NT):
                xt = xa[:, (tt % 2) * D:(tt % 2 + 1) * D]
                vector.wait_ge(ldx.h, 16 * (tt + 1) + ldx0)
                edge(vector.reduce_sum(sta[:, 4:5], xt, axis=AX.X))
                edge(vector.tensor_scalar(sta[:, 0:1], sta[:, 4:5], 1.0 / D, None,
                                          op0=ALU.mult))
                edge(vector.tensor_tensor(xnf[:], xt, xt, op=ALU.mult))
                edge(vector.reduce_sum(sta[:, 11:12], xnf[:], axis=AX.X))
                edge(vector.tensor_scalar(sta[:, 5:6], sta[:, 11:12], 1.0 / D, None,
                                          op0=ALU.mult))
                edge(vector.tensor_tensor(sta[:, 6:7], sta[:, 0:1], sta[:, 0:1],
                                          op=ALU.mult))
                vector.tensor_tensor(sta[:, 1:2], sta[:, 5:6], sta[:, 6:7],
                                     op=ALU.subtract).then_inc(vvar.h, vvar.bump(1))
                vector.wait_ge(srst.h, tt + 1)
                if tt > 0:
                    vector.wait_ge(atr.h, 4 * tt)
                    vector.wait_ge(adup.h, 32 * tt)
                    if stage == 1:
                        vector.wait_ge(aout.h, 16 * tt)
                edge(vector.tensor_scalar(xnf[:], xt, sta[:, 0:1], sta[:, 3:4],
                                          op0=ALU.subtract, op1=ALU.mult))
                vector.tensor_copy(xnb[:], xnf[:]).then_inc(axnb.h, axnb.bump(1))
                if stage >= 3:
                    if tt > 0:
                        vector.wait_ge(amw.h, 16 * tt)
                    vector.tensor_copy(xb[:], xt).then_inc(axb.h, axb.bump(1))
                edge(vector.reduce_sum(sta[:, 7:8], xnf[:, 0:D:2], axis=AX.X))
                edge(vector.reduce_sum(sta[:, 8:9], xnf[:, 1:D:2], axis=AX.X))
                edge(vector.tensor_tensor(sta[:, 9:10], sta[:, 7:8], sta[:, 8:9],
                                          op=ALU.subtract))
                edge(vector.tensor_tensor(sta[:, 10:11], sta[:, 9:10], sta[:, 9:10],
                                          op=ALU.mult))
                vector.tensor_scalar(pnyq[:, tt:tt + 1], sta[:, 10:11],
                                     1.0 / D, None, op0=ALU.mult)
                for r in range(4):
                    vector.wait_ge(atr.h, 4 * tt + r + 1)
                    src = pstA3[:, r % 2, :, :]
                    dst = xnt3[:, r * 8:(r + 1) * 8, tt * 128:(tt + 1) * 128]
                    vector.tensor_copy(dst, src).then_inc(aev.h, aev.bump(1))

        @blk.tensor
        def _(tensor):
            for tt in range(NT):
                tensor.wait_ge(axnb.h, tt + 1)
                for r in range(4):
                    gr = 4 * tt + r
                    if gr >= 2:
                        tensor.wait_ge(aev.h, gr - 1)
                    bank = pstA3[:, r % 2, :, :]
                    for k in range(8):
                        c = r * 8 + k
                        t = tensor.transpose(bank[:, k, :],
                                             xnb[:, c * 128:(c + 1) * 128],
                                             idb[:])
                        if k == 7:
                            t.then_inc(atr.h, atr.bump(1))

        if stage >= 2:
            @blk.sync
            def _(sync):
                for g in range(8):
                    for jh in range(2):
                        gh = 2 * g + jh
                        if gh >= 2:
                            sync.wait_ge(bmm.h, 2 * (gh - 1))
                        sync.dma_start(
                            out=cfb4[:, gh % 2, :, :],
                            in_=cfwd_d[g][jh * 2048:(jh + 1) * 2048, :]
                            .rearrange("(c p) w -> p c w", p=128)
                        ).then_inc(bldp[gh % 2].h, bldp[gh % 2].bump(16))
                if stage >= 3:
                    sync.wait_ge(sems["amw"].h, sems["amw"].v)
                sync.wait_ge(adup.h, adup.v)

            @blk.tensor
            def _(tensor):
                for ftg in range(16):
                    g, fhalf = divmod(ftg, 2)
                    for ts in range(2):
                        pi = ftg * 2 + ts
                        if pi >= 2:
                            tensor.wait_ge(bsq.h, pi - 1)
                        for j in range(NH):
                            jh = j // 16
                            gh = 2 * g + jh
                            tensor.wait_ge(bldp[gh % 2].h, 16 * (gh // 2 + 1))
                            tensor.wait_ge(aev.h,
                                           min(32, 16 * ts + 13 + j // 8))
                            mm = tensor.matmul(
                                psB4[:, ts * 1024:ts * 1024 + 512],
                                cfb4[:, gh % 2, j % 16,
                                     fhalf * 128:(fhalf + 1) * 128],
                                xnt3[:, j, ts * 512:(ts + 1) * 512],
                                start=(j == 0), stop=(j == NH - 1))
                            mm = tensor.matmul(
                                psB4[:, ts * 1024 + 512:(ts + 1) * 1024],
                                cfb4[:, gh % 2, j % 16,
                                     256 + fhalf * 128:256 + (fhalf + 1) * 128],
                                xnt3[:, j, ts * 512:(ts + 1) * 512],
                                start=(j == 0), stop=(j == NH - 1))
                        mm.then_inc(bmm.h, bmm.bump(1))

            @blk.scalar
            def _(scalar):
                for pi in range(32):
                    scalar.wait_ge(bmm.h, pi + 1)
                    if pi >= 2:
                        scalar.wait_ge(bev.h, pi - 1)
                    ts = pi % 2
                    sl = (pi % 2) * 1024
                    scalar.activation(sqb[:, sl:sl + 512],
                                      psB4[:, ts * 1024:ts * 1024 + 512],
                                      AF.Square)
                    scalar.activation(sqb[:, sl + 512:sl + 1024],
                                      psB4[:, ts * 1024 + 512:(ts + 1) * 1024],
                                      AF.Square).then_inc(bsq.h, bsq.bump(1))

            @blk.vector
            def _(vector):
                for pi in range(32):
                    vector.wait_ge(bsq.h, pi + 1)
                    ftg, ts = divmod(pi, 2)
                    sl = (pi % 2) * 1024
                    vector.tensor_tensor(
                        ppr3[:, ftg, ts * 512:(ts + 1) * 512],
                        sqb[:, sl:sl + 512],
                        sqb[:, sl + 512:sl + 1024],
                        op=ALU.add).then_inc(bev.h, bev.bump(1))

    if stage >= 2:
        vch = sems["vch"]
        if True:
            if True:
                cld, cmm, cev = sem("cld"), sem("cmm"), sem("cev")
                cwrp = [sem("cwr0"), sem("cwr1")]
                with (
                    nc.sbuf_tensor("c2q", [128, 16 * 1024], BF16) as c2q,
                    nc.sbuf_tensor("cst", [128, 2 * 1024], BF16) as cst,
                    nc.psum_tensor("psC", [128, 2048], F32) as psC,
                    nc.Block() as blk,
                ):
                    c2q3 = c2q[:].rearrange("p (fc l) -> p fc l", fc=16)

                    @blk.sync
                    def _(sync):
                        for q in range(4):
                            sync.wait_ge(cmm.h, 8 * q)
                            sync.dma_start(
                                out=c2q3,
                                in_=c2_d[q].rearrange("(c p) l -> p c l", p=128)
                            ).then_inc(cld.h, cld.bump(16))
                            for tt in range(NT):
                                gl = q * 8 + tt
                                sync.wait_ge(cev.h, gl + 1)
                                sync.dma_start(
                                    out=corr_d[tt * 128:(tt + 1) * 128,
                                               q * 1024:(q + 1) * 1024],
                                    in_=cst[:, (gl % 2) * 1024:
                                            (gl % 2 + 1) * 1024]
                                ).then_inc(cwrp[gl % 2].h, cwrp[gl % 2].bump(16))
                        sync.wait_ge(cwrp[0].h, 16 * 16)
                        sync.wait_ge(cwrp[1].h, 16 * 16)

                    @blk.tensor
                    def _(tensor):
                        for q in range(4):
                            tensor.wait_ge(cld.h, 16 * (q + 1))
                            for tt in range(NT):
                                gl = q * 8 + tt
                                if gl >= 2:
                                    tensor.wait_ge(cev.h, gl - 1)
                                for fc in range(16):
                                    lhsT = ppr3[:, fc, tt * 128:(tt + 1) * 128]
                                    for lt in range(2):
                                        b = (tt % 2) * 2 + lt
                                        mm = tensor.matmul(
                                            psC[:, b * 512:(b + 1) * 512],
                                            lhsT,
                                            c2q3[:, fc, lt * 512:(lt + 1) * 512],
                                            start=(fc == 0), stop=(fc == 15))
                                mm.then_inc(cmm.h, cmm.bump(1))

                    @blk.vector
                    def _(vector):
                        def edge(inst):
                            inst.then_inc(vch.h, vch.bump(1))
                            vector.wait_ge(vch.h, vch.v)

                        for q in range(4):
                            for tt in range(NT):
                                gl = q * 8 + tt
                                vector.wait_ge(cmm.h, gl + 1)
                                if gl >= 2:
                                    vector.wait_ge(cwrp[gl % 2].h, 16 * (gl // 2))
                                cb = (gl % 2) * 1024
                                for lt in range(2):
                                    b = (tt % 2) * 2 + lt
                                    ins = vector.tensor_copy(
                                        cst[:, cb + lt * 512:cb + (lt + 1) * 512],
                                        psC[:, b * 512:(b + 1) * 512])
                                edge(ins)
                                vector.tensor_scalar(
                                    cst[:, cb:cb + 1024:2],
                                    cst[:, cb:cb + 1024:2],
                                    pnyq[:, tt:tt + 1], None, op0=ALU.add)
                                vector.tensor_scalar(
                                    cst[:, cb + 1:cb + 1024:2],
                                    cst[:, cb + 1:cb + 1024:2],
                                    pnyq[:, tt:tt + 1], None, op0=ALU.subtract
                                ).then_inc(cev.h, cev.bump(1))

        ppr_cm.__exit__(None, None, None)
        if stage == 2:
            dxl, dini, dexp, dsm = sem("dxl"), sem("dini"), sem("dexp"), sem("dsm")
            did, giot, dga, dack, dwr = (sem("did"), sem("giot"), sem("dga"),
                                         sem("dack"), sem("dwr"))
            WOUT = 16
            if _DIAG:
                diag_d = nc.dram_tensor("diag", [R, 16], F32,
                                        kind="ExternalOutput").ap()
                ddg = sem("ddg")
            dcl = sem("dcl")
            with (
                nc.sbuf_tensor("crb2", [128, D], BF16) as crb2,
                nc.sbuf_tensor("dgs", [128, 16], F32) as dgs,
                nc.sbuf_tensor("m8", [128, 8], F32) as m8,
                nc.sbuf_tensor("f8", [128, 8], F32) as f8,
                nc.sbuf_tensor("e8f", [128, 8], F32) as e8f,
                nc.sbuf_tensor("p8", [128, 8], F32) as p8,
                nc.sbuf_tensor("d8", [128, 8], U32) as d8,
                nc.sbuf_tensor("idx8", [128, 16], U32) as idx8,
                nc.sbuf_tensor("rowb", [128, 1], F32) as rowb,
                nc.sbuf_tensor("gb", [128, 8192], BF16) as gb,
                nc.sbuf_tensor("acc", [128, D], F32) as acc,
                nc.sbuf_tensor("xf2", [128, 2 * D], F32) as xf2,
                nc.sbuf_tensor("s1r", [128, 2], F32) as s1r,
                nc.Block() as blk,
            ):
                xdup_flat = xdup_d.rearrange("r (c one) -> (r c) one", one=1)

                @blk.sync
                def _(sync):
                    for tt in range(NT):
                        if tt >= 1:
                            sync.wait_ge(did.h, tt)
                        sync.dma_start(out=crb2[:],
                                       in_=corr_d[tt * 128:(tt + 1) * 128, :]
                                       ).then_inc(dcl.h, dcl.bump(16))
                        if tt >= 2:
                            sync.wait_ge(dini.h, tt - 1)
                        sync.dma_start(out=xf2[:, (tt % 2) * D:(tt % 2 + 1) * D],
                                       in_=x_d[tt * 128:(tt + 1) * 128, :]
                                       ).then_inc(dxl.h, dxl.bump(16))
                        sync.wait_ge(dack.h, 8 * (tt + 1))
                        if _DIAG:
                            sync.dma_start(out=diag_d[tt * 128:(tt + 1) * 128, :],
                                           in_=dgs[:]).then_inc(ddg.h, ddg.bump(16))
                        sync.dma_start(out=out_d[tt * 128:(tt + 1) * 128, :],
                                       in_=acc[:]).then_inc(dwr.h, dwr.bump(16))

                @blk.scalar
                def _(scalar):
                    for tt in range(NT):
                        scalar.wait_ge(dxl.h, 16 * (tt + 1))
                        if tt > 0:
                            scalar.wait_ge(dwr.h, WOUT * tt)
                        scalar.activation(acc[:], xf2[:, (tt % 2) * D:(tt % 2 + 1) * D],
                                          AF.Identity).then_inc(dini.h, dini.bump(1))
                        scalar.wait_ge(dexp.h, tt + 1)
                        scalar.activation(e8f[:], f8[:], AF.Exp
                                          ).then_inc(dsm.h, dsm.bump(1))

                @blk.gpsimd
                def _(gpsimd):
                    for tt in range(NT):
                        gpsimd.wait_ge(did.h, tt)
                        gpsimd.iota(rowb[:], [[0, 1]], base=tt * 128 * 8192,
                                    channel_multiplier=8192,
                                    allow_small_or_imprecise_dtypes=True
                                    ).then_inc(giot.h, giot.bump(1))
                        for k in range(8):
                            gk = 8 * tt + k
                            gpsimd.wait_ge(did.h, tt + 1)
                            if gk >= 2:
                                gpsimd.wait_ge(dack.h, gk - 1)
                            gpsimd.indirect_dma_start(
                                out=gb[:, (k % 2) * D:(k % 2 + 1) * D],
                                out_offset=None,
                                in_=xdup_flat,
                                in_offset=bass.IndirectOffsetOnAxis(
                                    ap=idx8[:, (tt % 2) * 8 + k:(tt % 2) * 8 + k + 1],
                                    axis=0),
                            ).then_inc(dga.h, dga.bump(16))

                @blk.vector
                def _(vector):
                    def edge(inst):
                        inst.then_inc(vch.h, vch.bump(1))
                        vector.wait_ge(vch.h, vch.v)

                    for tt in range(NT):
                        cs = crb2[:]
                        vector.wait_ge(dcl.h, 16 * (tt + 1))
                        edge(vector.max(out=m8[:], in_=cs))
                        edge(vector.max_index(out=d8[:], in_max=m8[:], in_values=cs))
                        vector.wait_ge(giot.h, tt + 1)
                        vector.tensor_scalar(
                            idx8[:, (tt % 2) * 8:(tt % 2) * 8 + 8], d8[:],
                            rowb[:, 0:1], None, op0=ALU.add
                        ).then_inc(did.h, did.bump(1))
                        edge(vector.tensor_scalar(f8[:], m8[:], m8[:, 0:1], None,
                                                  op0=ALU.subtract))
                        vector.nop().then_inc(dexp.h, dexp.bump(1))
                        vector.wait_ge(dsm.h, tt + 1)
                        edge(vector.reduce_sum(s1r[:, 0:1], e8f[:], axis=AX.X))
                        edge(vector.reciprocal(s1r[:, 1:2], s1r[:, 0:1]))
                        edge(vector.tensor_scalar(p8[:], e8f[:], s1r[:, 1:2], None,
                                                  op0=ALU.mult))
                        if _DIAG:
                            if tt > 0:
                                vector.wait_ge(ddg.h, 16 * tt)
                            edge(vector.tensor_copy(dgs[:, 0:8], d8[:]))
                            edge(vector.tensor_copy(dgs[:, 8:16], p8[:]))
                        vector.wait_ge(dini.h, tt + 1)
                        for k in range(8):
                            gk = 8 * tt + k
                            vector.wait_ge(dga.h, 16 * (gk + 1))
                            edge(vector.scalar_tensor_tensor(
                                out=acc[:], in0=gb[:, (k % 2) * D:(k % 2 + 1) * D],
                                scalar=p8[:, k:k + 1], in1=acc[:],
                                op0=ALU.mult, op1=ALU.add))
                            vector.nop().then_inc(dack.h, dack.bump(1))

    if stage >= 3:
        vch = sems["vch"]
        eldp = [sem("eld0"), sem("eld1")]
        ezs, edqv, edqa = sem("ezs"), sem("edqv"), sem("edqa")
        etr, eevg, eevu, emm = sem("etr"), sem("eevg"), sem("eevu"), sem("emm")
        esil, ehm = sem("esil"), sem("ehm")
        ehwp = [sem("ehw0"), sem("ehw1")]
        mcl, mtk, mex, msm, mid = (sem("mcl"), sem("mtk"), sem("mex"),
                                   sem("msm"), sem("mid"))
        mgi, mga, mack, mwr = sem("mgi"), sem("mga"), sem("mack"), sem("mwr")
        mxl = sem("mxl")
        ectx = ExitStack()
        _e = ectx.enter_context
        wg2 = _e(nc.sbuf_tensor("wg2", [128, 2 * D], I32))
        wu2 = _e(nc.sbuf_tensor("wu2", [128, 2 * D], I32))
        guv2 = _e(nc.sbuf_tensor("guv2", [128, 256], I32))
        zsg = _e(nc.sbuf_tensor("zsg", [128, 64], F32))
        zsu = _e(nc.sbuf_tensor("zsu", [128, 64], F32))
        nzsu = _e(nc.sbuf_tensor("nzsu", [128, 64], F32))
        dqg = _e(nc.sbuf_tensor("dqg", [128, D], BF16))
        dqu = _e(nc.sbuf_tensor("dqu", [128, D], BF16))
        lg = _e(nc.sbuf_tensor("lg", [128, D], BF16))
        lu = _e(nc.sbuf_tensor("lu", [128, D], BF16))
        sg = _e(nc.sbuf_tensor("sg", [128, 1024], BF16))
        hb = _e(nc.sbuf_tensor("hb", [128, 2 * 1024], BF16))
        crb = _e(nc.sbuf_tensor("crb", [128, D], BF16))
        macc = _e(nc.sbuf_tensor("macc", [128, D], BF16))
        xsd = _e(nc.sbuf_tensor("xsd", [128, D], BF16))
        gb1 = _e(nc.sbuf_tensor("gb1", [128, D], BF16))
        mv8 = _e(nc.sbuf_tensor("mv8", [128, 32], F32))
        mi8 = _e(nc.sbuf_tensor("mi8", [128, 16], U32))
        mrow = _e(nc.sbuf_tensor("mrow", [128, 4], F32))
        pstE = _e(nc.psum_tensor("pstE", [128, 4 * 1024], BF16))
        psg = _e(nc.psum_tensor("psg", [128, 1024], F32))
        psu = _e(nc.psum_tensor("psu", [128, 1024], F32))
        with nc.Block() as blk:
            m8v = mv8[:, 0:8]
            f8v = mv8[:, 8:16]
            e8v = mv8[:, 16:24]
            p8v = mv8[:, 24:32]
            d8v = mi8[:, 0:8]
            ix8 = mi8[:, 8:16]
            s1v = mrow[:, 1:3]
            lg3 = lg[:].rearrange("p (c w) -> p c w", c=NH)
            lu3 = lu[:].rearrange("p (c w) -> p c w", c=NH)
            pstE3 = pstE[:].rearrange("p (b c w) -> p b c w", b=4, c=8)

            @blk.sync
            def _(sync):
                for i in range(NI):
                    s = i % 2
                    tt, ph = divmod(i, 14)
                    if tt < NT and ph == 0 and _MIX != "off":
                        sync.wait_ge(mtk.h, tt)
                        sync.dma_start(out=crb[:],
                                       in_=corr_d[tt * 128:(tt + 1) * 128, :]
                                       ).then_inc(mcl.h, mcl.bump(16))
                        if _MIX == "full":
                            if tt >= 1:
                                sync.wait_ge(mack.h, 8 * (tt - 1) + 1)
                            sync.dma_start(
                                out=xsd[:],
                                in_=mix_d[tt * 128:(tt + 1) * 128, :]
                            ).then_inc(mxl.h, mxl.bump(16))
                    if tt < NT and ph == 11 and _MIX == "full":
                        sync.wait_ge(mack.h, 8 * (tt + 1))
                        sync.dma_start(out=mix_d[tt * 128:(tt + 1) * 128, :],
                                       in_=macc[:]).then_inc(mwr.h,
                                                             mwr.bump(16))
                    if i >= 2:
                        sync.wait_ge(edqv.h, i - 1)
                        sync.wait_ge(edqa.h, i - 1)
                    sync.dma_start(out=wg2[:, s * D:(s + 1) * D],
                                   in_=gwT_d[i * 128:(i + 1) * 128, :]
                                   ).then_inc(eldp[s].h, eldp[s].bump(16))
                    sync.dma_start(out=wu2[:, s * D:(s + 1) * D],
                                   in_=uwT_d[i * 128:(i + 1) * 128, :]
                                   ).then_inc(eldp[s].h, eldp[s].bump(16))
                    sync.dma_start(out=guv2[:, s * 128:(s + 1) * 128],
                                   in_=guv_d[i * 128:(i + 1) * 128, :]
                                   ).then_inc(eldp[s].h, eldp[s].bump(16))
                if _MIX == "full":
                    sync.wait_ge(mwr.h, 16 * NT)

            @blk.vector
            def _(vector):
                def edge(inst):
                    inst.then_inc(vch.h, vch.bump(1))
                    vector.wait_ge(vch.h, vch.v)

                for i in range(NI):
                    s = i % 2
                    tt, ph = divmod(i, 14)
                    if tt < NT and _MIX != "off":
                        if ph == 1:
                            vector.wait_ge(mcl.h, 16 * (tt + 1))
                            edge(vector.max(out=m8v, in_=crb[:]))
                            edge(vector.max_index(out=d8v, in_max=m8v,
                                                  in_values=crb[:]))
                            vector.nop().then_inc(mtk.h, mtk.bump(1))
                            if _MIX != "topk":
                                vector.wait_ge(mgi.h, tt + 1)
                                if tt >= 1:
                                    vector.wait_ge(mga.h, 16 * 8 * tt)
                                vector.tensor_scalar(ix8, d8v, mrow[:, 0:1],
                                                     None, op0=ALU.add
                                                     ).then_inc(mid.h,
                                                                mid.bump(1))
                            edge(vector.tensor_scalar(f8v, m8v, m8v[:, 0:1],
                                                      None, op0=ALU.subtract))
                            vector.nop().then_inc(mex.h, mex.bump(1))
                        elif ph == 2:
                            vector.wait_ge(msm.h, tt + 1)
                            edge(vector.reduce_sum(s1v[:, 0:1], e8v, axis=AX.X))
                            edge(vector.reciprocal(s1v[:, 1:2], s1v[:, 0:1]))
                            edge(vector.tensor_scalar(p8v, e8v, s1v[:, 1:2],
                                                      None, op0=ALU.mult))
                        elif 3 <= ph <= 10 and _MIX in ("full", "acc"):
                            k = ph - 3
                            vector.wait_ge(mga.h, 16 * (8 * tt + k + 1))
                            if k == 0:
                                if _MIX == "full":
                                    if tt >= 1:
                                        vector.wait_ge(mwr.h, 16 * tt)
                                    vector.wait_ge(mxl.h, 16 * (tt + 1))
                                    edge(vector.scalar_tensor_tensor(
                                        out=macc[:], in0=gb1[:],
                                        scalar=p8v[:, 0:1], in1=xsd[:],
                                        op0=ALU.mult, op1=ALU.add))
                                else:
                                    edge(vector.tensor_scalar(
                                        macc[:], gb1[:], p8v[:, 0:1], None,
                                        op0=ALU.mult))
                            else:
                                edge(vector.scalar_tensor_tensor(
                                    out=macc[:], in0=gb1[:],
                                    scalar=p8v[:, k:k + 1], in1=macc[:],
                                    op0=ALU.mult, op1=ALU.add))
                            vector.nop().then_inc(mack.h, mack.bump(1))
                    sv = s * 32
                    vector.wait_ge(eldp[s].h, 48 * (i // 2 + 1))
                    gsv = guv2[:, s * 128:s * 128 + 32].bitcast(F32)
                    gzv = guv2[:, s * 128 + 32:s * 128 + 64]
                    usv = guv2[:, s * 128 + 64:s * 128 + 96].bitcast(F32)
                    uzv = guv2[:, s * 128 + 96:s * 128 + 128]
                    edge(vector.tensor_tensor(zsg[:, sv:sv + 32], gzv, gsv,
                                              op=ALU.mult))
                    edge(vector.tensor_tensor(zsu[:, sv:sv + 32], uzv, usv,
                                              op=ALU.mult))
                    vector.tensor_scalar(nzsu[:, sv:sv + 32], zsu[:, sv:sv + 32],
                                         -1.0, None,
                                         op0=ALU.mult).then_inc(ezs.h, ezs.bump(1))
                    if i > 0:
                        vector.wait_ge(etr.h, 8 * i - 4)
                    for c in range(NH):
                        ins = vector.tensor_scalar(
                            dqg[:, c * 128:(c + 1) * 128],
                            wg2[:, s * D + c * 128:s * D + (c + 1) * 128],
                            guv2[:, s * 128 + c:s * 128 + c + 1].bitcast(F32),
                            zsg[:, sv + c:sv + c + 1],
                            op0=ALU.mult, op1=ALU.subtract)
                    ins.then_inc(edqv.h, edqv.bump(1))
                    for r in range(4):
                        vector.wait_ge(etr.h, 8 * i + r + 1)
                        if r == 0 and i > 0:
                            vector.wait_ge(emm.h, 4 * i - 2)
                        vector.tensor_copy(lg3[:, r * 8:(r + 1) * 8, :],
                                           pstE3[:, r, :, :]
                                           ).then_inc(eevg.h, eevg.bump(1))
                    if i > 0:
                        for ts in range(2):
                            vector.wait_ge(esil.h, 2 * (i - 1) + ts + 1)
                            vector.wait_ge(emm.h, 4 * (i - 1) + 2 + ts + 1)
                            if i >= 3 and ts == 0:
                                vector.wait_ge(ehwp[(i - 3) % 2].h,
                                               16 * ((i - 3) // 2 + 1))
                            vector.tensor_tensor(
                                hb[:, ((i - 1) % 2) * 1024 + ts * 512:
                                   ((i - 1) % 2) * 1024 + (ts + 1) * 512],
                                sg[:, ts * 512:(ts + 1) * 512],
                                psu[:, ts * 512:(ts + 1) * 512],
                                op=ALU.mult).then_inc(ehm.h, ehm.bump(1))
                i = NI - 1
                for ts in range(2):
                    vector.wait_ge(esil.h, 2 * i + ts + 1)
                    vector.wait_ge(emm.h, 4 * i + 2 + ts + 1)
                    vector.tensor_tensor(
                        hb[:, (i % 2) * 1024 + ts * 512:
                           (i % 2) * 1024 + (ts + 1) * 512],
                        sg[:, ts * 512:(ts + 1) * 512],
                        psu[:, ts * 512:(ts + 1) * 512],
                        op=ALU.mult).then_inc(ehm.h, ehm.bump(1))

            @blk.scalar
            def _(scalar):
                def dequ(i):
                    s = i % 2
                    scalar.wait_ge(ezs.h, i + 1)
                    if i > 0:
                        scalar.wait_ge(etr.h, 8 * i)
                    for c in range(NH):
                        ins = scalar.activation(
                            dqu[:, c * 128:(c + 1) * 128],
                            wu2[:, s * D + c * 128:s * D + (c + 1) * 128],
                            AF.Identity,
                            scale=guv2[:, s * 128 + 64 + c:s * 128 + 64 + c + 1
                                       ].bitcast(F32),
                            bias=nzsu[:, s * 32 + c:s * 32 + c + 1])
                    ins.then_inc(edqa.h, edqa.bump(1))

                dequ(0)
                for i in range(NI):
                    tt, ph = divmod(i, 14)
                    if tt < NT and ph == 1 and _MIX != "off":
                        scalar.wait_ge(mex.h, tt + 1)
                        scalar.activation(e8v, f8v, AF.Exp
                                          ).then_inc(msm.h, msm.bump(1))
                    for ru in range(4):
                        scalar.wait_ge(etr.h, 8 * i + 4 + ru + 1)
                        if ru == 0 and i > 0:
                            scalar.wait_ge(emm.h, 4 * i)
                        scalar.activation(lu3[:, ru * 8:(ru + 1) * 8, :],
                                          pstE3[:, ru, :, :],
                                          AF.Identity).then_inc(eevu.h,
                                                                eevu.bump(1))
                    if i > 0:
                        scalar.wait_ge(ehm.h, 2 * i)
                        scalar.dma_start(
                            out=h_d[(i - 1) * 128:i * 128, :],
                            in_=hb[:, ((i - 1) % 2) * 1024:((i - 1) % 2 + 1) * 1024]
                            ).then_inc(ehwp[(i - 1) % 2].h,
                                       ehwp[(i - 1) % 2].bump(16))
                    if i < NI - 1:
                        dequ(i + 1)
                    for ts in range(2):
                        scalar.wait_ge(emm.h, 4 * i + ts + 1)
                        if i > 0 and ts == 0:
                            scalar.wait_ge(ehm.h, 2 * i)
                        scalar.activation(sg[:, ts * 512:(ts + 1) * 512],
                                          psg[:, ts * 512:(ts + 1) * 512],
                                          AF.Silu).then_inc(esil.h, esil.bump(1))
                scalar.wait_ge(ehm.h, 2 * NI)
                scalar.dma_start(
                    out=h_d[(NI - 1) * 128:NI * 128, :],
                    in_=hb[:, ((NI - 1) % 2) * 1024:((NI - 1) % 2 + 1) * 1024]
                    ).then_inc(ehwp[(NI - 1) % 2].h,
                               ehwp[(NI - 1) % 2].bump(16))
                scalar.wait_ge(ehwp[0].h, 16 * (NI // 2))
                scalar.wait_ge(ehwp[1].h, 16 * (NI // 2))

            @blk.gpsimd
            def _(gpsimd):
                if _MIX in ("off", "topk"):
                    return
                xdup_flat = xdup_d.rearrange("r (c one) -> (r c) one", one=1)
                for tt in range(NT):
                    gpsimd.wait_ge(mid.h, tt)
                    gpsimd.iota(mrow[:, 0:1], [[0, 1]], base=tt * 128 * 8192,
                                channel_multiplier=8192,
                                allow_small_or_imprecise_dtypes=True
                                ).then_inc(mgi.h, mgi.bump(1))
                    for k in range(8):
                        gk = 8 * tt + k
                        gpsimd.wait_ge(mid.h, tt + 1)
                        if gk >= 1 and _MIX in ("full", "acc"):
                            gpsimd.wait_ge(mack.h, gk)
                        gpsimd.indirect_dma_start(
                            out=gb1[:], out_offset=None,
                            in_=xdup_flat,
                            in_offset=bass.IndirectOffsetOnAxis(
                                ap=ix8[:, k:k + 1], axis=0),
                        ).then_inc(mga.h, mga.bump(16))


            @blk.tensor
            def _(tensor):
                for i in range(NI):
                    for r in range(8):
                        if r == 0:
                            tensor.wait_ge(edqv.h, i + 1)
                        if r == 4:
                            tensor.wait_ge(edqa.h, i + 1)
                        if r < 4:
                            if i > 0:
                                tensor.wait_ge(eevu.h, 4 * (i - 1) + r + 1)
                        else:
                            tensor.wait_ge(eevg.h, 4 * i + (r - 4) + 1)
                        src_t = dqg if r < 4 else dqu
                        rb = r % 4
                        for k in range(8):
                            c = rb * 8 + k
                            t = tensor.transpose(pstE3[:, r % 4, k, :],
                                                 src_t[:, c * 128:(c + 1) * 128],
                                                 idb[:])
                            if k == 7:
                                t.then_inc(etr.h, etr.bump(1))
                    tensor.wait_ge(eevg.h, 4 * i + 4)
                    if i > 0:
                        tensor.wait_ge(esil.h, 2 * i)
                    for ts in range(2):
                        for c in range(NH):
                            mm = tensor.matmul(psg[:, ts * 512:(ts + 1) * 512],
                                               lg3[:, c, :],
                                               xnt3[:, c, ts * 512:(ts + 1) * 512],
                                               start=(c == 0), stop=(c == NH - 1))
                        mm.then_inc(emm.h, emm.bump(1))
                    tensor.wait_ge(eevu.h, 4 * i + 4)
                    if i > 0:
                        tensor.wait_ge(ehm.h, 2 * i)
                    for ts in range(2):
                        for c in range(NH):
                            mm = tensor.matmul(psu[:, ts * 512:(ts + 1) * 512],
                                               lu3[:, c, :],
                                               xnt3[:, c, ts * 512:(ts + 1) * 512],
                                               start=(c == 0), stop=(c == NH - 1))
                        mm.then_inc(emm.h, emm.bump(1))

        ectx.close()

        xnt_cm.__exit__(None, None, None)
        if _SUB == "E":
            with nc.Block() as blk:
                @blk.sync
                def _(sync):
                    sync.wait_ge(sems["ehw0"].h, sems["ehw0"].v)
                    sync.wait_ge(sems["ehw1"].h, sems["ehw1"].v)
            ctx.close()
            return nc
        fdlp = [sem("fdl0"), sem("fdl1")]
        fw1p = [sem("fw10"), sem("fw11")]
        fzs, fqv, fqa = sem("fzs"), sem("fqv"), sem("fqa")
        fqv2 = sem("fqv2")
        FG = 8
        ftr, fevv, feva = sem("ftr"), sem("fevv"), sem("feva")
        fwrp = [sem("fwr0"), sem("fwr1")]
        with (
            nc.sbuf_tensor("dwq", [128, 4 * 7168], I32) as dwq,
            nc.sbuf_tensor("dqd", [128, 4 * 7168], BF16) as dqd,
            nc.sbuf_tensor("dlb", [128, 2 * 7168], BF16) as dlb,
            nc.sbuf_tensor("dsv", [128, 2 * NI], F32) as dsv,
            nc.sbuf_tensor("dzv", [128, 2 * NI], I32) as dzv,
            nc.sbuf_tensor("zsd", [128, 2 * NI], F32) as zsd,
            nc.sbuf_tensor("nzsd", [128, 2 * NI], F32) as nzsd,
            nc.psum_tensor("pstF", [128, 4 * 1024], BF16) as pstF,
            nc.Block() as blk,
        ):
            pstF3 = pstF[:].rearrange("p (b c w) -> p b c w", b=4, c=8)
            dwq4 = dwq[:].rearrange("p (s w) -> p s w", s=4)
            dqd4 = dqd[:].rearrange("p (s w) -> p s w", s=4)
            dlb2 = dlb[:].rearrange("p (s w) -> p s w", s=2)

            @blk.sync
            def _(sync):
                for fi in range(32):
                    s = fi % 2
                    if fi >= 2:
                        sync.wait_ge(fqv.h, fi - 1)
                        sync.wait_ge(fqv2.h, fi - 1)
                        sync.wait_ge(fqa.h, fi - 1)
                    sync.dma_start(out=dsv[:, s * NI:(s + 1) * NI],
                                   in_=dsT_d[fi * 128:(fi + 1) * 128, :]
                                   ).then_inc(fdlp[s].h, fdlp[s].bump(16))
                    sync.dma_start(out=dzv[:, s * NI:(s + 1) * NI],
                                   in_=dzT_d[fi * 128:(fi + 1) * 128, :]
                                   ).then_inc(fdlp[s].h, fdlp[s].bump(16))
                    for hf in range(2):
                        if fi >= 2:
                            if hf == 0:
                                sync.wait_ge(fqv.h, fi - 1)
                            else:
                                sync.wait_ge(fqv2.h, fi - 1)
                                sync.wait_ge(fqa.h, fi - 1)
                        sync.dma_start(
                            out=dwq4[:, s * 2 + hf, :],
                            in_=dwT_d[fi * 128:(fi + 1) * 128,
                                      hf * 7168:(hf + 1) * 7168]
                            ).then_inc((fdlp if hf == 0 else fw1p)[s].h,
                                       (fdlp if hf == 0 else fw1p)[s].bump(16))
                    if fi >= 1:
                        sync.wait_ge(fevv.h, 7 * fi)
                        sync.dma_start(out=dl_d[fi - 1][:, 0:7168],
                                       in_=dlb2[:, 0, :]
                                       ).then_inc(fwrp[0].h, fwrp[0].bump(16))
                        sync.wait_ge(feva.h, 7 * fi)
                        sync.dma_start(out=dl_d[fi - 1][:, 7168:2 * 7168],
                                       in_=dlb2[:, 1, :]
                                       ).then_inc(fwrp[1].h, fwrp[1].bump(16))
                sync.wait_ge(fevv.h, 7 * 32)
                sync.dma_start(out=dl_d[31][:, 0:7168], in_=dlb2[:, 0, :]
                               ).then_inc(fwrp[0].h, fwrp[0].bump(16))
                sync.wait_ge(feva.h, 7 * 32)
                sync.dma_start(out=dl_d[31][:, 7168:2 * 7168], in_=dlb2[:, 1, :]
                               ).then_inc(fwrp[1].h, fwrp[1].bump(16))
                sync.wait_ge(fwrp[0].h, 16 * 32)
                sync.wait_ge(fwrp[1].h, 16 * 32)

            @blk.vector
            def _(vector):
                def edge(inst):
                    inst.then_inc(vch.h, vch.bump(1))
                    vector.wait_ge(vch.h, vch.v)

                for fi in range(32):
                    s = fi % 2
                    vector.wait_ge(fdlp[s].h, 48 * (fi // 2 + 1))
                    edge(vector.tensor_tensor(zsd[:, s * NI:(s + 1) * NI],
                                              dzv[:, s * NI:(s + 1) * NI],
                                              dsv[:, s * NI:(s + 1) * NI],
                                              op=ALU.mult))
                    vector.tensor_scalar(nzsd[:, s * NI:(s + 1) * NI],
                                         zsd[:, s * NI:(s + 1) * NI], -1.0, None,
                                         op0=ALU.mult).then_inc(fzs.h, fzs.bump(1))
                    if fi >= 2:
                        vector.wait_ge(ftr.h, 14 * (fi - 2) + 7)
                    for j in range(56):
                        ins = vector.tensor_scalar(
                            dqd4[:, s * 2, j * 128:(j + 1) * 128],
                            dwq4[:, s * 2, j * 128:(j + 1) * 128],
                            dsv[:, s * NI + j:s * NI + j + 1],
                            zsd[:, s * NI + j:s * NI + j + 1],
                            op0=ALU.mult, op1=ALU.subtract)
                    ins.then_inc(fqv.h, fqv.bump(1))
                    vector.wait_ge(fw1p[s].h, 16 * (fi // 2 + 1))
                    if fi >= 2:
                        vector.wait_ge(ftr.h, 14 * (fi - 1))
                    for j in range(FG):
                        ins = vector.tensor_scalar(
                            dqd4[:, s * 2 + 1, j * 128:(j + 1) * 128],
                            dwq4[:, s * 2 + 1, j * 128:(j + 1) * 128],
                            dsv[:, s * NI + 56 + j:s * NI + 56 + j + 1],
                            zsd[:, s * NI + 56 + j:s * NI + 56 + j + 1],
                            op0=ALU.mult, op1=ALU.subtract)
                    ins.then_inc(fqv2.h, fqv2.bump(1))
                    for r in range(7):
                        gr = 14 * fi + r
                        vector.wait_ge(ftr.h, gr + 1)
                        if r == 0 and fi >= 1:
                            vector.wait_ge(fwrp[0].h, 16 * fi)
                        vector.tensor_copy(
                            dlb2[:, 0, r * 1024:(r + 1) * 1024]
                            .rearrange("p (c w) -> p c w", c=8),
                            pstF3[:, gr % 4, :, :]
                            ).then_inc(fevv.h, fevv.bump(1))

            @blk.scalar
            def _(scalar):
                for fi in range(32):
                    s = fi % 2
                    scalar.wait_ge(fzs.h, fi + 1)
                    scalar.wait_ge(fw1p[s].h, 16 * (fi // 2 + 1))
                    if fi >= 2:
                        scalar.wait_ge(ftr.h, 14 * (fi - 1))
                    for j in range(FG, 56):
                        ins = scalar.activation(
                            dqd4[:, s * 2 + 1, j * 128:(j + 1) * 128],
                            dwq4[:, s * 2 + 1, j * 128:(j + 1) * 128],
                            AF.Identity,
                            scale=dsv[:, s * NI + 56 + j:s * NI + 56 + j + 1],
                            bias=nzsd[:, s * NI + 56 + j:s * NI + 56 + j + 1])
                    ins.then_inc(fqa.h, fqa.bump(1))
                    for r in range(7, 14):
                        gr = 14 * fi + r
                        scalar.wait_ge(ftr.h, gr + 1)
                        if r == 7 and fi >= 1:
                            scalar.wait_ge(fwrp[1].h, 16 * fi)
                        scalar.activation(dlb2[:, 1, (r - 7) * 1024:(r - 6) * 1024]
                                          .rearrange("p (c w) -> p c w", c=8),
                                          pstF3[:, gr % 4, :, :],
                                          AF.Identity).then_inc(feva.h,
                                                                feva.bump(1))

            @blk.tensor
            def _(tensor):
                for fi in range(32):
                    s = fi % 2
                    for r in range(14):
                        gr = 14 * fi + r
                        if r == 0:
                            tensor.wait_ge(fqv.h, fi + 1)
                        if r == 7:
                            tensor.wait_ge(fqv2.h, fi + 1)
                        if r == 8:
                            tensor.wait_ge(fqa.h, fi + 1)
                        if gr >= 4:
                            fi2, r2 = divmod(gr - 4, 14)
                            if r2 < 7:
                                tensor.wait_ge(fevv.h, 7 * fi2 + r2 + 1)
                            else:
                                tensor.wait_ge(feva.h, 7 * fi2 + (r2 - 7) + 1)
                        hf = 0 if r < 7 else 1
                        for k in range(8):
                            j = (r - 7 * hf) * 8 + k
                            t = tensor.transpose(
                                pstF3[:, gr % 4, k, :],
                                dqd4[:, s * 2 + hf, j * 128:(j + 1) * 128],
                                idb[:])
                            if k == 7:
                                t.then_inc(ftr.h, ftr.bump(1))

        NS = 16
        NB = 14
        fllp = [sem("fll0"), sem("fll1"), sem("fll2"), sem("fll3")]
        fhlp = [sem("fhl0"), sem("fhl1"), sem("fhl2"), sem("fhl3")]
        fmm, foe = sem("fmm"), sem("foe")
        fot, foa = sem("fot"), sem("foa")
        fmip = [sem("fmi0"), sem("fmi1")]
        fowp = [sem("fow0"), sem("fow1")]
        B = {"fll": [2 * s for s in range(NS + 2)],
             "fhl": [NB * s for s in range(NS + 2)],
             "fmm": [NB * s for s in range(NS + 2)],
             "foe": [4 * s for s in range(NS + 2)],
             "fot": [8 * s for s in range(NS + 2)],
             "fmi": [8 * s for s in range(NS + 2)],
             "foa": [8 * s for s in range(NS + 2)],
             "fow": [8 * s for s in range(NS + 2)]}
        with (
            nc.sbuf_tensor("DLp", [128, 2 * 2 * DI], BF16) as DLp,
            nc.sbuf_tensor("hhp", [128, 4 * 8192], BF16) as hh,
            nc.sbuf_tensor("o2p", [128, 2 * 2048], F32) as o2sb,
            nc.sbuf_tensor("mxp", [128, 2 * 256], BF16) as mxsb,
            nc.sbuf_tensor("oup", [128, 2 * 256], F32) as outsb,
            nc.psum_tensor("psFp", [128, 4096], F32) as psF,
            nc.Block() as blk,
        ):
            DL4 = DLp[:].rearrange("p (sl ft w) -> p sl ft w", sl=2, ft=2)
            hh3 = hh[:].rearrange("p (b c t) -> p b c t", b=4, c=8)
            o22 = o2sb[:].rearrange("p (sl w) -> p sl w", sl=2)

            @blk.gpsimd
            def _(gpsimd):
                for s in range(NS):
                    for tb in range(8):
                        if B["foa"][s] + tb - 1 > 0:
                            gpsimd.wait_ge(foa.h, B["foa"][s] + tb - 1)
                        gpsimd.dma_start(
                            out=mxsb[:, (tb % 2) * 256:(tb % 2 + 1) * 256],
                            in_=mix_d[tb * 128:(tb + 1) * 128,
                                      s * 256:(s + 1) * 256]
                        ).then_inc(fmip[tb % 2].h, fmip[tb % 2].bump(16))
                        gpsimd.wait_ge(foa.h, B["foa"][s] + tb + 1)
                        gpsimd.dma_start(
                            out=out_d[tb * 128:(tb + 1) * 128,
                                      s * 256:(s + 1) * 256],
                            in_=outsb[:, (tb % 2) * 256:(tb % 2 + 1) * 256]
                        ).then_inc(fowp[tb % 2].h, fowp[tb % 2].bump(16))

            @blk.sync
            def _(sync):
                for ft in range(2):
                    sync.dma_start(out=DL4[:, 0, ft, :], in_=dl_d[ft]
                                   ).then_inc(fllp[ft].h, fllp[ft].bump(16))
                for s in range(NS):
                    for bi in range(NB):
                        gb = s * NB + bi
                        if gb >= 4:
                            s2, b2 = divmod(gb - 4, NB)
                            sync.wait_ge(fmm.h, B["fmm"][s2] + b2 + 1)
                        sync.dma_start(
                            out=hh3[:, gb % 4, :, :],
                            in_=h_d[bi * 1024:(bi + 1) * 1024, :]
                            .rearrange("(c p) t -> p c t", p=128)
                        ).then_inc(fhlp[gb % 4].h, fhlp[gb % 4].bump(16))
                        if bi == 1 and s + 1 < NS:
                            if s >= 1:
                                sync.wait_ge(fmm.h, B["fmm"][s])
                            for ft in range(2):
                                q2 = ((s + 1) % 2) * 2 + ft
                                sync.dma_start(out=DL4[:, (s + 1) % 2, ft, :],
                                               in_=dl_d[2 * (s + 1) + ft]
                                               ).then_inc(fllp[q2].h,
                                                          fllp[q2].bump(16))


            @blk.scalar
            def _(scalar):
                for s in range(NS):
                    scalar.wait_ge(fmm.h, B["fmm"][s] + NB)
                    if s >= 2:
                        scalar.wait_ge(fot.h, B["fot"][s - 2] + 8)
                    sb = (s % 2) * 2048
                    for b in (0, 2, 1, 3):
                        scalar.activation(o22[:, s % 2, b * 512:(b + 1) * 512],
                                          psF[:, sb + b * 512:sb + (b + 1) * 512],
                                          AF.Identity).then_inc(foe.h, foe.bump(1))

            @blk.tensor
            def _(tensor):
                def mms(s, tail_fn=None):
                    sb = (s % 2) * 2048
                    if s >= 2:
                        tensor.wait_ge(foa.h, B["foa"][s - 2] + 8)
                    tensor.wait_ge(fhlp[(s * NB) % 4].h,
                                   16 * ((s * NB) // 4 + 1))
                    for ft in range(2):
                        tensor.wait_ge(fllp[(s % 2) * 2 + ft].h,
                                       16 * (s // 2 + 1))
                        for ci in range(8):
                            for ts in range(2):
                                b = ft * 2 + ts
                                tensor.matmul(
                                    psF[:, sb + b * 512:sb + (b + 1) * 512],
                                    DL4[:, s % 2, ft, ci * 128:(ci + 1) * 128],
                                    hh3[:, (s * NB) % 4, ci,
                                        ts * 512:(ts + 1) * 512],
                                    start=(ci == 0), stop=False)
                    for ci in range(8, NI):
                        bi = ci // 8
                        gb = s * NB + bi
                        tensor.wait_ge(fhlp[gb % 4].h, 16 * (gb // 4 + 1))
                        for ts in range(2):
                            for ft in range(2):
                                b = ft * 2 + ts
                                mm = tensor.matmul(
                                    psF[:, sb + b * 512:sb + (b + 1) * 512],
                                    DL4[:, s % 2, ft, ci * 128:(ci + 1) * 128],
                                    hh3[:, gb % 4, ci % 8,
                                        ts * 512:(ts + 1) * 512],
                                    start=False, stop=(ci == NI - 1))
                        if ci % 8 == 7:
                            inc = 2 if ci == 15 else 1
                            mm.then_inc(fmm.h, fmm.bump(inc))
                        if ci == 23 and tail_fn is not None:
                            tail_fn()

                def tails(s):
                    sb = (s % 2) * 2048
                    for tb in range(8):
                        ts = tb // 4
                        tensor.wait_ge(foe.h, B["foe"][s] + 4)
                        for ft in range(2):
                            src = o22[:, s % 2,
                                      (ft * 2 + ts) * 512 + (tb % 4) * 128:
                                      (ft * 2 + ts) * 512 + (tb % 4 + 1) * 128]
                            t = tensor.transpose(
                                psF[:, sb + (tb // 2) * 512 + (tb % 2) * 256
                                    + ft * 128:
                                    sb + (tb // 2) * 512 + (tb % 2) * 256
                                    + (ft + 1) * 128],
                                src, idf_s[:])
                        t.then_inc(fot.h, fot.bump(1))

                mms(0)
                for s in range(1, NS):
                    mms(s, tail_fn=lambda s2=s - 1: tails(s2))
                tails(NS - 1)

            @blk.vector
            def _(vector):
                def edge(inst):
                    inst.then_inc(vch.h, vch.bump(1))
                    vector.wait_ge(vch.h, vch.v)

                for s in range(NS):
                    sb = (s % 2) * 2048
                    for tb in range(8):
                        vector.wait_ge(fot.h, B["fot"][s] + (tb | 1) + 1)
                        vector.wait_ge(fmip[tb % 2].h,
                                       16 * ((8 * s + tb) // 2 + 1))
                        wo = 8 * s + tb - 2
                        if wo >= 0:
                            vector.wait_ge(fowp[wo % 2].h, 16 * (wo // 2 + 1))
                        edge(vector.tensor_tensor(
                            outsb[:, (tb % 2) * 256:(tb % 2 + 1) * 256],
                            psF[:, sb + (tb // 2) * 512 + (tb % 2) * 256:
                                sb + (tb // 2) * 512 + (tb % 2 + 1) * 256],
                            mxsb[:, (tb % 2) * 256:(tb % 2 + 1) * 256],
                            op=ALU.add))
                        vector.nop().then_inc(foa.h, foa.bump(1))

    if stage < 3:
        if stage == 1:
            ppr_cm.__exit__(None, None, None)
        xnt_cm.__exit__(None, None, None)
    with nc.Block() as blk:
        @blk.sync
        def _(sync):
            if stage == 1:
                sync.wait_ge(sems["aout"].h, sems["aout"].v)
            if stage == 2:
                sync.wait_ge(sems["dwr"].h, sems["dwr"].v)
            if stage >= 3 and _SUB != "P":
                sync.wait_ge(sems["fow0"].h, sems["fow0"].v)
                sync.wait_ge(sems["fow1"].h, sems["fow1"].v)
                sync.wait_ge(sems["ehw0"].h, sems["ehw0"].v)
                sync.wait_ge(sems["ehw1"].h, sems["ehw1"].v)
            sync.wait_ge(sems["adup"].h, sems["adup"].v)
            sync.wait_ge(sems["aev"].h, sems["aev"].v)

    ctx.close()
    return nc


def _prep_host(inputs):
    x = np.ascontiguousarray(np.asarray(inputs["L0_attn_x"], dtype=np.float32)
                             ).reshape(8192, D)
    cons = {}
    cons["idb"] = np.eye(128, dtype=np.float32).astype(ml_dtypes.bfloat16)
    cons["idf"] = np.eye(128, dtype=np.float32)
    j = np.arange(D)[:, None].astype(np.float64)
    cfwd = np.empty((8, D, 512), dtype=ml_dtypes.bfloat16)
    for g in range(8):
        f = (g * 256 + np.arange(256))[None, :].astype(np.float64)
        ang = 2.0 * np.pi * j * f / D
        cfwd[g, :, 0:256] = np.cos(ang).astype(ml_dtypes.bfloat16)
        cfwd[g, :, 256:512] = np.sin(ang).astype(ml_dtypes.bfloat16)
    cons["cfwd"] = cfwd
    c2 = np.empty((4, NF2, 1024), dtype=ml_dtypes.bfloat16)
    f = np.arange(NF2)[:, None].astype(np.float64)
    w = np.full((NF2, 1), 2.0 / D); w[0, 0] = 1.0 / D
    for q in range(4):
        lag = (q * 1024 + np.arange(1024))[None, :].astype(np.float64)
        c2[q] = (w * np.cos(2.0 * np.pi * f * lag / D)).astype(ml_dtypes.bfloat16)
    cons["c2"] = c2
    cons["gwT"] = np.ascontiguousarray(np.asarray(inputs["L0_ffn_gate_w"]).T)
    cons["uwT"] = np.ascontiguousarray(np.asarray(inputs["L0_ffn_up_w"]).T)
    cons["dwT"] = np.ascontiguousarray(np.asarray(inputs["L0_ffn_down_w"]).T)
    gsT = np.ascontiguousarray(np.asarray(inputs["L0_ffn_gate_s"]).T)
    usT = np.ascontiguousarray(np.asarray(inputs["L0_ffn_up_s"]).T)
    gzT = np.ascontiguousarray(np.asarray(inputs["L0_ffn_gate_z"]).T)
    uzT = np.ascontiguousarray(np.asarray(inputs["L0_ffn_up_z"]).T)
    cons["guv"] = np.ascontiguousarray(np.concatenate(
        [gsT.view(np.int32), gzT, usT.view(np.int32), uzT], axis=1))
    cons["dsT"] = np.ascontiguousarray(np.asarray(inputs["L0_ffn_down_s"]).T)
    cons["dzT"] = np.ascontiguousarray(np.asarray(inputs["L0_ffn_down_z"]).T)
    return x, cons


def kernel(**inputs):
    x, cons = _prep_host(inputs)
    nc = build_program(_STAGE)
    names = {"idb", "idf", "cfwd", "c2", "gwT", "uwT", "dwT",
             "guv", "dsT", "dzT"}
    if _STAGE < 3:
        for k in ["idf", "gwT", "uwT", "dwT", "guv", "dsT", "dzT"]:
            names.discard(k)
    if _STAGE < 2:
        for k in ["cfwd", "c2"]:
            names.discard(k)
    in_maps = []
    for c in range(NCORES):
        m = {"x": np.ascontiguousarray(x[c * R:(c + 1) * R])}
        for k in names:
            m[k] = cons[k]
        in_maps.append(m)
    res = run_bass_kernel_spmd(nc, in_maps, core_ids=list(range(NCORES)))
    out = np.concatenate([res.results[c]["out"] for c in range(NCORES)], axis=0)
    return out.reshape(4, 2048, D).astype(np.float32)
